# version 17
# baseline (speedup 1.0000x reference)
import os
import numpy as np
import ml_dtypes
import concourse.bass as bass
import concourse.mybir as mybir
import concourse.bass_utils as _bass_utils
from concourse.bass_utils import run_bass_kernel_spmd

if not getattr(_bass_utils, "_mk_ldw_patched", False):
    _orig_run_command = _bass_utils.run_command

    def _mk_run_command(argv, **kwargs):
        argv = ["--enable-ldw-opt=true" if a == "--enable-ldw-opt=false" else a
                for a in argv]
        return _orig_run_command(argv, **kwargs)

    _bass_utils.run_command = _mk_run_command
    _bass_utils._mk_ldw_patched = True

F32 = mybir.dt.float32
BF16 = mybir.dt.bfloat16
I32 = mybir.dt.int32
AF = mybir.ActivationFunctionType
ALU = mybir.AluOpType

NCORES = 8
T = 2048
E = 2048
NH = 16
HD = 128
O = 5504
OC = O // 128
ECN = E // 128
TTN = T // 128
EPC = E // NCORES
OPC = O // NCORES
INV_SQRT_D = 1.0 / float(np.sqrt(HD))
KB = 1024

DEV_STOP = os.environ.get("MK_DEV_STOP", "")
DEV_SER = os.environ.get("MK_DEV_SER", "")


class Sem:
    def __init__(self, es, nc, name):
        self.h = es.enter_context(nc.semaphore(name))
        self.n = 0


class Rec:
    ENGS = ("sync", "gpsimd", "vector", "scalar", "tensor")

    def __init__(self):
        self.ops = {e: [] for e in self.ENGS}
        self.all_sems = []
        self.nc = None
        self.phase = "init"

    def op(self, eng, emit, waits=(), inc=None, incn=1, fence=False):
        if DEV_SER == "1":
            waits = list(waits) + [(s, s.n) for s in self.all_sems if s is not inc]
        elif DEV_SER == "sy" and eng == "sync" and inc is not None:
            waits = list(waits) + [(inc, inc.n)]
        wl = [(s.h, int(v)) for (s, v) in waits if v > 0]
        if inc is not None:
            inc.n += incn
        fb = self.fence_bufs.get(eng) if fence else None

        def run(e, wl=wl, emit=emit, inc=inc, incn=incn, fb=fb):
            for (h, v) in wl:
                e.wait_ge(h, v)
            inst = emit(e)
            if fb is not None:
                inst = e.tensor_copy(fb[0], fb[1]) if not isinstance(fb, tuple)                     else fb[0](e)
            if inc is not None:
                inst.then_inc(inc.h, incn)
        self.ops[eng].append(run)

    def emit_all(self, block):
        for eng in self.ENGS:
            lst = self.ops[eng]

            def f(e, lst=lst):
                for r in lst:
                    r(e)
            getattr(block, eng)(f)


def build_nc():
    nc = bass.Bass("TRN2", target_bir_lowering=False)

    def din(name, shape, dt):
        return nc.dram_tensor(name, list(shape), dt, kind="ExternalInput").ap()

    x_in = din("x_in", (T, E), F32)
    wqT = din("wqT", (E, E), F32)
    wkT = din("wkT", (E, E), F32)
    wvT = din("wvT", (E, E), F32)
    woT = din("woT", (E, E), F32)
    gpiece = din("gpiece", (EPC, O), I32)
    upiece = din("upiece", (EPC, O), I32)
    dpiece = din("dpiece", (OPC, E), I32)
    lutb_d = din("lutb", (128, 16), F32)
    c_ident = din("c_ident", (128, 128), BF16)
    c_ones = din("c_ones", (128, 128), BF16)
    c_ones1 = din("c_ones1", (1, 128), F32)
    c_maskT = din("c_maskT", (128, 128), F32)
    c_rot64 = din("c_rot64", (128, 128), BF16)
    c_cosT = din("c_cosT", (128, T), BF16)
    c_sinTf = din("c_sinTf", (128, T), BF16)

    out = nc.dram_tensor("out", [T, E], F32, kind="ExternalOutput").ap()

    xb1 = nc.dram_tensor("xb1", [T, E], F32).ap()
    xb2 = nc.dram_tensor("xb2", [T, E], F32).ap()
    xnT_sc = nc.dram_tensor("xnT_sc", [ECN * 128, T], BF16).ap()
    q_sc = nc.dram_tensor("q_sc", [NH * 128, T], BF16).ap()
    k_sc = nc.dram_tensor("k_sc", [NH * 128, T], BF16).ap()
    v_sc = nc.dram_tensor("v_sc", [NH * T, HD], BF16).ap()
    GUW = [1536, 1536, 1536, 896]
    cc_gp = [nc.dram_tensor(f"cc_g{j}", [EPC, w], BF16).ap()
             for j, w in enumerate(GUW)]
    cc_up = [nc.dram_tensor(f"cc_u{j}", [EPC, w], BF16).ap()
             for j, w in enumerate(GUW)]
    cc_d = nc.dram_tensor("cc_d", [OPC, E], BF16).ap()
    wg_p = [nc.dram_tensor(f"wg_sc{j}", [E, w], BF16, addr_space="Shared").ap()
            for j, w in enumerate(GUW)]
    wu_p = [nc.dram_tensor(f"wu_sc{j}", [E, w], BF16, addr_space="Shared").ap()
            for j, w in enumerate(GUW)]
    wd_sc = nc.dram_tensor("wd_sc", [O, E], BF16, addr_space="Shared").ap()

    from contextlib import ExitStack
    es = ExitStack()
    E_ = es.enter_context

    AB1 = 88 * KB
    AB2 = 88 * KB
    AMID = 16 * KB
    ACON = 3 * KB
    AWRK = 256 + 12 * KB
    a_b1 = E_(nc.sbuf_tensor("a_b1", [128, AB1], mybir.dt.uint8))
    a_b2 = E_(nc.sbuf_tensor("a_b2", [128, AB2], mybir.dt.uint8))
    a_mid = E_(nc.sbuf_tensor("a_mid", [128, AMID], mybir.dt.uint8))
    a_con = E_(nc.sbuf_tensor("a_con", [128, ACON], mybir.dt.uint8))
    a_wrk = E_(nc.sbuf_tensor("a_wrk", [128, AWRK], mybir.dt.uint8))

    def view(arena, off, dt, cols, parts=128):
        return arena[0:parts, off:off + cols * mybir.dt.size(dt)].bitcast(dt)

    sb_ident = view(a_con, 0, BF16, 128)
    sb_ones = view(a_con, 256, BF16, 128)
    sb_ones1 = view(a_con, 512, F32, 128, parts=1)
    sb_maskT = view(a_con, 1024, F32, 128)
    sb_lutb = view(a_con, 1536, F32, 16)
    sb_rot = view(a_con, 1792, BF16, 128)
    f_ve = view(a_con, 2048, F32, 4)
    f_sc = view(a_con, 2112, F32, 4)
    f_gp = view(a_con, 2176, F32, 4)

    w_stats = [view(a_wrk, i * 16, F32, 1) for i in range(16)]
    w_t1 = [view(a_wrk, 256 + q * 2 * KB, BF16, 1024) for q in range(2)]
    w_scrap = view(a_wrk, 256 + 4 * KB, F32, 1024)
    w_t2 = [view(a_wrk, 256 + 8 * KB + q * 2 * KB, BF16, 1024) for q in range(2)]

    m_wg = [view(a_mid, b * 8 * KB, BF16, 16 * 256) for b in range(2)]
    m_x2k = [view(a_mid, b * 8 * KB, F32, 2048) for b in range(2)]
    m_x1k = [view(a_mid, b * 4 * KB, F32, 1024) for b in range(2)]

    def h_half(c):
        return view(a_b1, c * 2 * KB, BF16, 1024)
    aT = [view(a_b1, h * 4 * KB, BF16, T) for h in range(NH)]
    sb_cos = view(a_wrk, 256, BF16, T)
    sb_sin = view(a_wrk, 256 + 8 * KB, BF16, T)
    w_lsb = view(a_b1, 72 * KB, F32, 1024)
    w_lrc = view(a_b1, 76 * KB, F32, 1024)
    al_x2k = [view(a_b1, 64 * KB + b_ * 8 * KB, F32, 2048) for b_ in range(2)]
    al_xnt = [view(a_b1, 80 * KB + b_ * 4 * KB, BF16, 2048) for b_ in range(2)]

    xnT = [view(a_b2, ec * 4 * KB, BF16, T) for ec in range(ECN)]
    xnT_all = a_b2[0:128, 0:64 * KB].bitcast(BF16)
    m_wub = [view(a_b2, 64 * KB + b * 8 * KB, BF16, 16 * 256) for b in range(2)]
    xn_t = [view(a_b2, 80 * KB + p * 4 * KB, BF16, 2048) for p in range(2)]

    def wd_f(c):
        return view(a_b2, c * 2 * KB, BF16, 1024)
    def ptil(kc):
        return view(a_b2, kc * 2 * KB, BF16, 1024)
    qh = [view(a_b2, 32 * KB + p * 4 * KB, BF16, T) for p in range(2)]
    kh = [view(a_b2, 40 * KB + p * 4 * KB, BF16, T) for p in range(2)]
    vh = [view(a_b2, 48 * KB + p * 4 * KB, BF16, 16 * 128) for p in range(2)]
    w3 = [view(a_b2, 64 * KB + p * 12 * KB, BF16, 3 * 16 * 128) for p in range(2)]
    qraw = view(a_b1, 0, BF16, T)
    vtmp = view(a_b1, 4 * KB, BF16, T)
    qrot = view(a_b1, 8 * KB, BF16, T)
    qf = [view(a_b1, 12 * KB + i * 4 * KB, BF16, T) for i in range(2)]
    kf = [view(a_b1, 20 * KB + i * 4 * KB, BF16, T) for i in range(2)]
    vs2 = [view(a_b1, 28 * KB + i * 4 * KB, BF16, T) for i in range(2)]
    wob = [view(a_b2, ec * 4 * KB, BF16, T) for ec in range(ECN)]

    pA = E_(nc.psum_tensor("pA", [128, 2048], F32))
    pB = E_(nc.psum_tensor("pB", [128, 2048], F32))
    pT_ = [pA, pB]

    ag = Sem(es, nc, "ag")
    pe = Sem(es, nc, "pe")
    ve = Sem(es, nc, "ve")
    sc = Sem(es, nc, "sc")
    gp = Sem(es, nc, "gp")
    s_cn = Sem(es, nc, "s_cn")
    s_ig = Sem(es, nc, "s_ig")
    s_iu = Sem(es, nc, "s_iu")
    s_id = Sem(es, nc, "s_id")
    s_id2 = Sem(es, nc, "s_id2")
    s_cg = Sem(es, nc, "s_cg")
    s_cu = Sem(es, nc, "s_cu")
    s_cd = Sem(es, nc, "s_cd")
    s_xl = [Sem(es, nc, "s_xl0"), Sem(es, nc, "s_xl1")]
    s_sp = Sem(es, nc, "s_sp")
    s_rl = Sem(es, nc, "s_rl")
    s_wg = [Sem(es, nc, "s_wg0"), Sem(es, nc, "s_wg1")]
    s_wd = Sem(es, nc, "s_wd")
    s_xr = [Sem(es, nc, "s_xr0"), Sem(es, nc, "s_xr1")]
    s_xs = Sem(es, nc, "s_xs")
    s_w3 = [Sem(es, nc, "s_w30"), Sem(es, nc, "s_w31")]
    s_wo = Sem(es, nc, "s_wo")
    s_qkv = Sem(es, nc, "s_qkv")
    s_hd = [Sem(es, nc, "s_hd0"), Sem(es, nc, "s_hd1")]

    R = Rec()
    R.nc = nc
    R.fence_bufs = {
        "vector": ((lambda e: e.tensor_copy(f_ve[:, 0:2], f_ve[:, 2:4])),),
        "scalar": ((lambda e: e.copy(f_sc[:, 0:2], f_sc[:, 2:4])),),
        "gpsimd": ((lambda e: e.tensor_copy(f_gp[:, 0:2], f_gp[:, 2:4])),),
    }
    R.all_sems = [ag, pe, ve, sc, gp, s_cn, s_ig, s_iu, s_id, s_id2, s_cg, s_cu, s_cd,
                  s_xl[0], s_xl[1], s_sp, s_rl, s_wg[0], s_wg[1], s_wd,
                  s_xr[0], s_xr[1], s_xs, s_w3[0], s_w3[1], s_wo, s_qkv,
                  s_hd[0], s_hd[1]]
    block = E_(nc.Block())

    for (dst, src) in [(sb_ident, c_ident), (sb_ones, c_ones), (sb_ones1, c_ones1),
                       (sb_maskT, c_maskT), (sb_lutb, lutb_d), (sb_rot, c_rot64)]:
        R.op("sync", (lambda d, s: (lambda e: e.dma_start(out=d, in_=s)))(dst, src),
             inc=s_cn, incn=16)
    consts_cn = s_cn.n
    R.op("vector", lambda e: e.memset(w_stats[11], 1e-5), inc=ve)

    def lookup_ops(e, src_i32, dst_bf, ib, scr, np_):
        last = None
        e.tensor_copy(ib[0:np_, :], src_i32[0:np_, :])
        for v in range(16):
            if v == 0:
                last = e.tensor_scalar(dst_bf[0:np_, :], ib[0:np_, :], 0.0,
                                       sb_lutb[0:np_, 0:1], ALU.is_equal, ALU.mult)
            else:
                e.tensor_scalar(scr[0:np_, :], ib[0:np_, :], float(v),
                                sb_lutb[0:np_, v:v + 1], ALU.is_equal, ALU.mult)
                last = e.tensor_add(dst_bf[0:np_, :], dst_bf[0:np_, :],
                                    scr[0:np_, :])
        return last

    GU_PARTS = [(0, 1536), (1536, 3072), (3072, 4608), (4608, O)]
    PW = 1536
    mb_idx = [view(a_b1, 64 * KB + q * 12 * KB, I32, 2 * PW) for q in range(2)]
    mw_ib = view(a_b2, 80 * KB, BF16, PW)
    mw_w = view(a_b2, 80 * KB + PW * 2, BF16, PW)
    mw_scr = w_scrap.bitcast(BF16)

    def lookup_ops(e, src_i32, dst_bf, ib, scr, np_):
        last = None
        e.tensor_copy(ib[0:np_, :], src_i32[0:np_, :])
        for v in range(16):
            if v == 0:
                last = e.tensor_scalar(dst_bf[0:np_, :], ib[0:np_, :], 0.0,
                                       sb_lutb[0:np_, 0:1], ALU.is_equal, ALU.mult)
            else:
                e.tensor_scalar(scr[0:np_, :], ib[0:np_, :], float(v),
                                sb_lutb[0:np_, v:v + 1], ALU.is_equal, ALU.mult)
                last = e.tensor_add(dst_bf[0:np_, :], dst_bf[0:np_, :],
                                    scr[0:np_, :])
        return last

    def mat_gu(ln_ve, deferred):
        seq = []
        for j, (c0, c1) in enumerate(GU_PARTS):
            seq.append(("g", j, c0, c1, gpiece, cc_gp[j], wg_p[j], s_cg))
            seq.append(("u", j, c0, c1, upiece, cc_up[j], wu_p[j], s_cu))
        lk_hist = {}
        prev_store = [None]
        ln_pe = pe.n
        ld_of = {}

        def idx_load(si, lw):
            mn, j, c0, c1, piece, cc, wsc_, scc = seq[si]
            q = si % 2
            w = c1 - c0
            siq = s_ig if q == 0 else s_iu
            R.op("sync", (lambda piece=piece, q=q, c0=c0, c1=c1, w=w:
                          (lambda e: e.dma_start(
                out=mb_idx[q].rearrange("p (a n) -> p a n", n=PW)[:, :, 0:w],
                in_=piece[:, c0:c1].rearrange("(a p) n -> p a n", p=128))))(),
                waits=lw, inc=siq, incn=16)
            ld_of[si] = (siq, siq.n)
        idx_load(0, [(s_cn, consts_cn)])
        idx_load(1, [])
        for si, (mn, j, c0, c1, piece, cc, wsc_, scc) in enumerate(seq):
            q = si % 2
            w = c1 - c0
            ld = ld_of[si]
            for ai in range(2):
                lkw = [ld] if ai == 0 else []
                if si == 0 and ai == 0:
                    lkw += [(ve, ln_ve), (pe, ln_pe)]
                if prev_store[0] is not None:
                    lkw.append(prev_store[0])
                R.op("vector", (lambda q=q, ai=ai, w=w: (lambda e: lookup_ops(
                    e, mb_idx[q].rearrange("p (a n) -> p a n", n=PW)[:, ai, 0:w],
                    mw_w[:, 0:w], mw_ib[:, 0:w], mw_scr[:, 0:w], 128)))(),
                    waits=lkw, inc=ve, fence=True)
                lk = ve.n
                R.op("sync", (lambda cc=cc, ai=ai, w=w:
                              (lambda e: e.dma_start(
                                  out=cc[ai * 128:(ai + 1) * 128, 0:w],
                                  in_=mw_w[:, 0:w])))(),
                    waits=[(ve, lk)], inc=scc, incn=16)
                prev_store[0] = (scc, scc.n)
            if si + 2 < len(seq):
                idx_load(si + 2, [(ve, ve.n)])
            lk_hist[si] = ve.n
            st = scc.n

            def mk_ag(cc=cc, wsc_=wsc_, scc=scc, st=st):
                R.op("gpsimd", lambda e: e.collective_compute(
                    "AllGather", ALU.bypass,
                    replica_groups=[list(range(NCORES))],
                    ins=[cc[:]], outs=[wsc_[:]]),
                    waits=[(scc, st)], inc=ag)
            if j == 0:
                mk_ag()
            else:
                deferred.append(mk_ag)
        return ve.n, prev_store[0]

    def mat_d(gu_ve, gu_store):
        db_ib = view(a_b2, 80 * KB, BF16, 2048)
        db_w = view(a_b2, 84 * KB, BF16, 2048)
        nstr = 6
        lk_hist = {}
        prev_store = gu_store
        dld = {}

        def d_load(i, lw):
            q = i % 2
            r0 = i * 128
            r1 = min(OPC, r0 + 128)
            nr = r1 - r0
            sdq = s_id if q == 0 else s_id2
            R.op("sync", (lambda q=q, r0=r0, r1=r1, nr=nr: (lambda e: e.dma_start(
                out=view(a_b1, 64 * KB + q * 12 * KB, I32, 2048)[0:nr, :],
                in_=dpiece[r0:r1, :])))(),
                waits=lw, inc=sdq, incn=16)
            dld[i] = (sdq, sdq.n)
        d_load(0, [(ve, gu_ve)])
        d_load(1, [])
        for i in range(nstr):
            q = i % 2
            r0 = i * 128
            r1 = min(OPC, r0 + 128)
            nr = r1 - r0
            R.op("vector", (lambda q=q, nr=nr: (lambda e: lookup_ops(
                e, view(a_b1, 64 * KB + q * 12 * KB, I32, 2048), db_w,
                db_ib, mw_scr, nr)))(),
                waits=[dld[i], prev_store], inc=ve, fence=True)
            lk_hist[i] = ve.n
            R.op("sync", (lambda r0=r0, r1=r1, nr=nr: (lambda e: e.dma_start(
                out=cc_d[r0:r1, :], in_=db_w[0:nr, :])))(),
                waits=[(ve, lk_hist[i])], inc=s_cd, incn=16)
            prev_store = (s_cd, s_cd.n)
            if i + 2 < nstr:
                d_load(i + 2, [(ve, lk_hist[i])])
        st = s_cd.n

        def d_ag():
            R.op("gpsimd", lambda e: e.collective_compute(
                "AllGather", ALU.bypass, replica_groups=[list(range(NCORES))],
                ins=[cc_d[:]], outs=[wd_sc[:]]),
                waits=[(s_cd, st)], inc=ag)
        return st, ve.n, d_ag

    def ln_phase(x_src, spill, evac_extra=None, x_ready=0, ready_map=None,
                 xbuf=None, nbuf=None, half1_eng="sync", half2_eng="gpsimd"):
        xb_ = xbuf if xbuf is not None else m_x2k
        nb_ = nbuf if nbuf is not None else xn_t
        evac_sc = [0, 0]
        norm_ve = [0, 0]
        vst_v = [0, 0]
        pe_entry_ve = ve.n
        pe_entry_sc = sc.n
        for tt in range(TTN):
            p = tt % 2
            w = [(ve, norm_ve[p])] if tt >= 2 else []
            if ready_map is not None:
                w.append((s_xs, ready_map[tt]))
            elif tt == 0 and x_ready:
                w.append((s_xs, x_ready))
            R.op(half1_eng, (lambda tt=tt, p=p: (lambda e: e.dma_start(
                out=xb_[p][:, 0:1024],
                in_=x_src[tt * 128:(tt + 1) * 128, 0:1024])))(),
                waits=w, inc=s_xl[p], incn=16)
            wv = [(ve, norm_ve[p])] if tt >= 2 else []
            if ready_map is not None:
                wv.append((s_xs, ready_map[tt]))
            elif tt == 0 and x_ready:
                wv.append((s_xs, x_ready))
            R.op(half2_eng, (lambda tt=tt, p=p: (lambda e: e.dma_start(
                out=xb_[p][:, 1024:2048],
                in_=x_src[tt * 128:(tt + 1) * 128, 1024:2048])))(),
                waits=wv, inc=s_xl[p], incn=16)
            ld = s_xl[p].n
            s_sum, s_sq, s_mean, s_var, s_rstd = (w_stats[p * 5 + i] for i in range(5))
            hi_sum, hi_sq = w_stats[12 + p], w_stats[14 + p]
            sw = [(s_xl[p], ld)]
            if tt >= 2:
                sw.append((ve, norm_ve[p]))
            else:
                sw.append((ve, pe_entry_ve))
            def stats_s(e, p=p, s_sum=s_sum, s_sq=s_sq, s_mean=s_mean,
                        s_var=s_var, s_rstd=s_rstd, hi_sum=hi_sum, hi_sq=hi_sq):
                e.activation(w_scrap[:, 0:1024], xb_[p][:, 0:1024], AF.Identity,
                             accum_out=s_sum)
                e.activation(w_scrap[:, 0:1024], xb_[p][:, 1024:2048],
                             AF.Identity, accum_out=hi_sum)
                e.activation(w_scrap[:, 0:1024], xb_[p][:, 0:1024], AF.Square,
                             accum_out=s_sq)
                e.activation(w_scrap[:, 0:1024], xb_[p][:, 1024:2048],
                             AF.Square, accum_out=hi_sq)
                e.add(s_sum, s_sum, hi_sum)
                e.add(s_sq, s_sq, hi_sq)
                e.mul(s_mean, s_sum, 1.0 / E)
                e.mul(s_var, s_sq, 1.0 / E)
                e.activation(w_stats[10], s_mean, AF.Square)
                e.mul(w_stats[10], w_stats[10], -1.0)
                e.add(s_var, s_var, w_stats[10])
                e.add(s_var, s_var, w_stats[11])
                e.activation(w_stats[10], s_var, AF.Ln)
                return e.activation(s_rstd, w_stats[10], AF.Exp, scale=-0.5)
            R.op("scalar", (lambda f=stats_s: (lambda e: f(e)))(), waits=sw, inc=sc)
            rst = sc.n
            vst_v[p] = 0
            R.op("vector", (lambda p=p, s_mean=s_mean, s_rstd=s_rstd:
                            (lambda e: e.tensor_scalar(
                                nb_[p][:], xb_[p][:], s_mean, s_rstd,
                                ALU.subtract, ALU.mult)))(),
                 waits=[(sc, rst)], inc=ve, fence=True)
            norm_ve[p] = ve.n
            w = [(ve, norm_ve[p])]
            if tt >= 2:
                w.append((sc, evac_sc[p]))
            elif pe_entry_ve or pe_entry_sc:
                w += [(ve, pe_entry_ve), (sc, pe_entry_sc)]
            def trans(e, p=p):
                last = None
                for ec in range(ECN):
                    last = e.matmul(pT_[p][:, ec * 128:(ec + 1) * 128],
                                    nb_[p][:, ec * 128:(ec + 1) * 128],
                                    sb_ident[:], start=True, stop=True)
                return last
            R.op("tensor", (lambda f=trans: (lambda e: f(e)))(), waits=w, inc=pe)
            tmm = pe.n
            ew = [(pe, tmm)]
            if tt < 2 and evac_extra is not None:
                ew.append(evac_extra)
            def evac(e, p=p, tt=tt):
                src = pT_[p][:].rearrange("p (ec t) -> p ec t", t=128)
                dst = xnT_all.rearrange("p (ec t) -> p ec t", t=T)[
                    :, :, tt * 128:(tt + 1) * 128]
                return e.copy(dst, src)
            R.op("scalar", (lambda f=evac: (lambda e: f(e)))(), waits=ew, inc=sc,
                 fence=True)
            evac_sc[p] = sc.n
        ln_sc = sc.n
        ln_ve = ve.n
        if spill:
            for ec in range(ECN):
                R.op("sync", (lambda ec=ec: (lambda e: e.dma_start(
                    out=xnT_sc[ec * 128:(ec + 1) * 128, :], in_=xnT[ec][:])))(),
                    waits=[(sc, ln_sc)] if ec == 0 else [], inc=s_sp, incn=16)
        return ln_sc, ln_ve, s_sp.n

    def ffn_body(x_src, x_dst, ln_sc, ln_ve, entry_ve, h32_extra=None,
                 gp_hooks=None):
        tile_done = {}
        for Hh in range(2):
            if Hh == 1:
                relw = [(pe, pe.n), (s_sp, s_sp.n)]
                for ec in range(ECN):
                    R.op("sync", (lambda ec=ec: (lambda e: e.dma_start(
                        out=xnT[ec][:], in_=xnT_sc[ec * 128:(ec + 1) * 128, :])))(),
                        waits=relw if ec == 0 else [], inc=s_rl, incn=16)
                entry_pe = [(ag, 2), (sc, sc.n), (ve, ve.n), (s_rl, s_rl.n)]
            else:
                entry_pe = [(ag, 2), (sc, ln_sc), (ve, entry_ve)]
            R.phase = R.phase.split("/")[0] + f"/gu{Hh}"
            NB = (OC + 1) // 2
            h_ve = {}
            wld = {}
            blk_pe = {}

            def load_gu(b):
                ww = []
                if b >= 2:
                    ww.append((pe, blk_pe[b - 2]))
                if b == 0:
                    ww.append((ve, ln_ve))
                    if Hh == 1:
                        ww.append((s_xs, s_xs.n))
                part = min(3, (2 * b + 1) // 12)
                ww.append((ag, 2 * (part + 1)))
                n0 = 2 * b * 128
                n1 = min(O, n0 + 256)
                wdt = n1 - n0
                cj0 = n0 - 1536 * part
                for (buf, wsc_) in ((m_wg, wg_p), (m_wub, wu_p)):
                    R.op("scalar", (lambda buf=buf, wt=wsc_[part], b=b,
                                  cj0=cj0, wdt=wdt: (lambda e: e.dma_start(
                        out=buf[b % 2][:].rearrange("p (a n) -> p a n", n=256)[:, :, 0:wdt],
                        in_=wt[:, cj0:cj0 + wdt].rearrange("(a p) n -> p a n", p=128))))(),
                        waits=ww, inc=s_wg[b % 2], incn=16)
                    ww = []
                wld[b] = (s_wg[b % 2], s_wg[b % 2].n)
            load_gu(0)
            load_gu(1)
            for c in range(OC):
                b, half = c // 2, c % 2
                q = c % 2
                pw = list(entry_pe) if c == 0 else []
                pw.append(wld[b])
                if c >= 2:
                    pw.append(h_ve[c - 2])
                def wmk(buf, b, half, q, ps_off, Hh):
                    def f(e):
                        wv = buf[b % 2][:].rearrange("p (a n) -> p a n", n=256)
                        last = None
                        for n in range(2):
                            for ec in range(ECN):
                                last = e.matmul(
                                    pT_[q][:, ps_off + n * 512:ps_off + n * 512 + 512],
                                    wv[:, ec, half * 128:half * 128 + 128],
                                    xnT[ec][:, Hh * 1024 + n * 512:Hh * 1024 + n * 512 + 512],
                                    start=(ec == 0), stop=(ec == ECN - 1))
                        return last
                    return f
                R.op("tensor", wmk(m_wg, b, half, q, 0, Hh), waits=pw, inc=pe)
                g_pe = pe.n
                R.op("tensor", wmk(m_wub, b, half, q, 1024, Hh), inc=pe)
                u_pe = pe.n
                blk_pe[b] = u_pe
                R.op("scalar", (lambda q=q: (lambda e: e.activation(
                    w_t1[q][:], pT_[q][:, 0:1024], AF.Silu)))(),
                    waits=[(pe, g_pe)], inc=sc, fence=True)
                R.op("scalar", (lambda q=q: (lambda e: e.copy(
                    w_t2[q][:], pT_[q][:, 1024:2048])))(),
                    waits=[(pe, u_pe)], inc=sc, fence=True)
                silu_sc = sc.n
                hww = [(sc, silu_sc)]
                if c == 32 and h32_extra is not None:
                    hww += h32_extra
                R.op("gpsimd", (lambda c=c, q=q: (lambda e: e.tensor_tensor(
                    h_half(c)[:], w_t1[q][:], w_t2[q][:], ALU.mult)))(),
                    waits=hww, inc=gp, incn=1, fence=True)
                h_ve[c] = (gp, gp.n)
                if half == 1 and b + 2 < NB:
                    load_gu(b + 2)
                if Hh == 0 and gp_hooks and c in gp_hooks:
                    for fn in gp_hooks[c]:
                        fn()
            gu_gp = gp.n
            R.phase = R.phase.split("/")[0] + f"/dn{Hh}"
            if DEV_STOP == "h0" and Hh == 0:
                for cch in range(16):
                    R.op("gpsimd", (lambda cch=cch: (lambda e: e.dma_start(
                        out=out[cch * 128:(cch + 1) * 128, 0:1024],
                        in_=h_half(cch)[:])))(),
                        waits=[(gp, gu_gp)] if cch == 0 else [], inc=gp, incn=16)
                R.ops["sync"].append(lambda e: [e.wait_ge(s.h, s.n) for s in R.all_sems if s.n])
                R.emit_all(block)
                es.close()
                return "STOP"
            last_psum = (gp, gu_gp)
            gate_end = pe.n
            prev_stage = None
            dn_st = {}
            for Ff in range(2):
                wd_sy = {}
                for c in range(OC):
                    ww = []
                    if prev_stage is None:
                        if c == 0:
                            ww = [(ag, 9), (pe, pe.n)]
                    else:
                        rel = {0: 11, 11: 22, 22: 33, 33: OC}.get(c)
                        if rel is not None:
                            ww = [(pe, prev_stage[rel])]
                    R.op("scalar", (lambda c=c, Ff=Ff: (lambda e: e.dma_start(
                        out=wd_f(c)[:],
                        in_=wd_sc[c * 128:(c + 1) * 128, Ff * 1024:(Ff + 1) * 1024])))(),
                        waits=ww, inc=s_wd, incn=16)
                    wd_sy[c] = s_wd.n
                y_ve = {}
                stage_pe = {}
                for tt in range(8):
                    r = tt % 2
                    gt = Hh * 8 + tt
                    g = Ff * 8 + tt
                    ww = ([(s_xs, dn_st[g - 2])] if g >= 2
                          else [(pe, gate_end)])
                    R.op("sync", (lambda gt=gt, r=r, Ff=Ff: (lambda e: e.dma_start(
                        out=m_x1k[r][:],
                        in_=x_src[gt * 128:(gt + 1) * 128, Ff * 1024:(Ff + 1) * 1024])))(),
                        waits=ww, inc=s_xr[r], incn=16)
                    xld = (s_xr[r], s_xr[r].n)
                    pw = [(ve, y_ve[tt - 2])] if tt >= 2 else [last_psum]
                    def ymk(tt, r, wd_sy, c0, c1):
                        def f(e):
                            last = None
                            for c in range(c0, c1):
                                if tt == 0 and c0 == 0 and c % 4 == 0:
                                    e.wait_ge(s_wd.h, wd_sy[min(c + 3, OC - 1)])
                                for n in range(2):
                                    last = e.matmul(
                                        pT_[r][:, n * 512:(n + 1) * 512],
                                        h_half(c)[:, tt * 128:(tt + 1) * 128],
                                        wd_f(c)[:, n * 512:(n + 1) * 512],
                                        start=(c == 0), stop=(c == OC - 1))
                            return last
                        return f
                    if tt < 7:
                        R.op("tensor", ymk(tt, r, wd_sy, 0, OC), waits=pw, inc=pe)
                    else:
                        for (c0, c1) in ((0, 11), (11, 22), (22, 33), (33, OC)):
                            R.op("tensor", ymk(tt, r, wd_sy, c0, c1),
                                 waits=pw if c0 == 0 else [], inc=pe)
                            stage_pe[c1] = pe.n
                    ymm = pe.n
                    if DEV_STOP == "y0":
                        R.op("vector", (lambda r=r: (lambda e: e.tensor_scalar_mul(
                            m_x1k[r][:], pT_[r][:, 0:1024], 1.0)))(),
                            waits=[(pe, ymm), xld], inc=ve)
                    else:
                        R.op("vector", (lambda r=r: (lambda e: e.scalar_tensor_tensor(
                            m_x1k[r][:], pT_[r][:, 0:1024], 0.5, m_x1k[r][:],
                            ALU.mult, ALU.add)))(),
                            waits=[(pe, ymm), xld], inc=ve, fence=True)
                    y_ve[tt] = ve.n
                    stw = [(ve, y_ve[tt])]
                    if g >= 1:
                        stw.append((s_xs, dn_st[g - 1]))
                    R.op("gpsimd", (lambda gt=gt, r=r, Ff=Ff: (lambda e: e.dma_start(
                        out=x_dst[gt * 128:(gt + 1) * 128, Ff * 1024:(Ff + 1) * 1024],
                        in_=m_x1k[r][:])))(),
                        waits=stw, inc=s_xs, incn=16)
                    dn_st[g] = s_xs.n
                    if Ff == 1:
                        tile_done[gt] = s_xs.n
                last_psum = (ve, ve.n)
                prev_stage = stage_pe
        return tile_done

    def attn_call(x_src, x_dst, ready_map):
        R.phase = "aln"
        R.op("sync", lambda e: e.dma_start(out=sb_cos, in_=c_cosT),
             waits=[(gp, gp.n)], inc=s_cn, incn=16)
        R.op("sync", lambda e: e.dma_start(out=sb_sin, in_=c_sinTf),
             inc=s_cn, incn=16)
        rope_cn = s_cn.n
        ln_sc, ln_ve, _ = ln_phase(x_src, spill=False, x_ready=s_xs.n)
        R.phase = "a2"
        sc_vt = {}
        sc_vs = {}
        pe_v = {}
        ve_rope = {}
        sy_st = {}
        a2_entry_pe = pe.n
        wld_of = {}

        def load_w3(h):
            p = h % 2
            ww = [(pe, pe_v[h - 2])] if h >= 2 else [(pe, a2_entry_pe)]
            for wi, wt in enumerate((wqT, wkT, wvT)):
                R.op("gpsimd", (lambda wt=wt, p=p, wi=wi, h=h: (lambda e: e.dma_start(
                    out=w3[p][:].rearrange("p (w a n) -> p w a n", w=3, n=128)[:, wi],
                    in_=wt[:, h * 128:(h + 1) * 128].rearrange("(a p) n -> p a n", p=128))))(),
                    waits=ww if wi == 0 else [], inc=s_w3[p], incn=16)
                ww = []
            wld_of[h] = (s_w3[p], s_w3[p].n)
        load_w3(0)
        for h in range(NH):
            p = h % 2
            wld = wld_of[h]

            def pmk(p, wi, dst):
                def f(e):
                    wv = w3[p][:].rearrange("p (w a n) -> p w a n", w=3, n=128)
                    last = None
                    for n in range(4):
                        for ec in range(ECN):
                            last = e.matmul(dst[:, n * 512:(n + 1) * 512],
                                            wv[:, wi, ec, :],
                                            xnT[ec][:, n * 512:(n + 1) * 512],
                                            start=(ec == 0), stop=(ec == ECN - 1))
                    return last
                return f

            def rotmk(dst, srcb):
                def f(e):
                    last = None
                    for n in range(4):
                        last = e.matmul(dst[:, n * 512:(n + 1) * 512], sb_rot[:],
                                        srcb[:, n * 512:(n + 1) * 512],
                                        start=True, stop=True)
                    return last
                return f

            pw = [wld, (sc, sc_vt[h - 1] if h >= 1 else ln_sc)]
            R.op("tensor", pmk(p, 0, pA), waits=pw, inc=pe)
            q_pe = pe.n
            qw = [(pe, q_pe)]
            if h >= 1:
                qw.append((ve, ve_rope[("k", h - 1)]))
            R.op("scalar", lambda e: e.copy(qraw[:], pA[:]), waits=qw, inc=sc,
                 fence=True)
            qr = sc.n
            kw = [(sc, sc_vs[h - 1])] if h >= 1 else [(sc, ln_sc)]
            R.op("tensor", pmk(p, 1, pB), waits=kw, inc=pe)
            k_pe = pe.n
            if h + 1 < NH:
                load_w3(h + 1)
            R.op("tensor", rotmk(pA, qraw), waits=[(sc, qr)], inc=pe)
            rq_pe = pe.n
            R.op("scalar", lambda e: e.copy(qrot[:], pA[:]),
                 waits=[(pe, rq_pe)], inc=sc, fence=True)
            qrot_sc = sc.n
            def ropemk(dstf):
                def f(e):
                    e.tensor_tensor(vtmp[:], qrot[:], sb_sin[:], ALU.mult)
                    e.tensor_tensor(dstf[:], qraw[:], sb_cos[:], ALU.mult)
                    return e.tensor_add(dstf[:], dstf[:], vtmp[:])
                return f
            rw = [(sc, qrot_sc), (s_cn, rope_cn)]
            if h >= 1:
                rw.append((pe, pe_v[h - 1]))
            if h >= 2:
                rw.append(sy_st[("q", h - 2)])
            R.op("vector", ropemk(qf[p]), waits=rw, inc=ve, fence=True)
            ve_rope[("q", h)] = ve.n
            R.op("sync", (lambda h=h, p=p: (lambda e: e.dma_start(
                out=q_sc[h * 128:(h + 1) * 128, :], in_=qf[p][:])))(),
                waits=[(ve, ve.n)], inc=s_qkv, incn=16)
            sy_st[("q", h)] = (s_qkv, s_qkv.n)
            R.op("scalar", lambda e: e.copy(qraw[:], pB[:]),
                 waits=[(pe, k_pe), (ve, ve_rope[("q", h)])], inc=sc, fence=True)
            kr = sc.n
            R.op("tensor", rotmk(pB, qraw), waits=[(sc, kr)], inc=pe)
            rk_pe = pe.n
            R.op("tensor", pmk(p, 2, pA), waits=[(sc, qrot_sc)], inc=pe)
            v_pe = pe.n
            R.op("scalar", lambda e: e.copy(qrot[:], pB[:]),
                 waits=[(pe, rk_pe), (ve, ve_rope[("q", h)])], inc=sc, fence=True)
            krot_sc = sc.n
            krw = [(sc, krot_sc)]
            if h >= 2:
                krw.append(sy_st[("k", h - 2)])
            R.op("vector", ropemk(kf[p]), waits=krw, inc=ve, fence=True)
            ve_rope[("k", h)] = ve.n
            R.op("sync", (lambda h=h, p=p: (lambda e: e.dma_start(
                out=k_sc[h * 128:(h + 1) * 128, :], in_=kf[p][:])))(),
                waits=[(ve, ve.n)], inc=s_qkv, incn=16)
            sy_st[("k", h)] = (s_qkv, s_qkv.n)
            R.op("scalar", lambda e: e.copy(vtmp[:], pA[:]),
                 waits=[(pe, v_pe), (ve, ve_rope[("k", h)])], inc=sc, fence=True)
            sc_vt[h] = sc.n

            def vtrans(e):
                last = None
                for t2 in range(TTN):
                    last = e.matmul(pB[:, t2 * 128:(t2 + 1) * 128],
                                    vtmp[:, t2 * 128:(t2 + 1) * 128],
                                    sb_ident[:], start=True, stop=True)
                return last
            R.op("tensor", lambda e: vtrans(e),
                 waits=[(sc, sc_vt[h])], inc=pe)
            pe_v[h] = pe.n
            vsw = [(pe, pe_v[h])]
            if h >= 2:
                vsw.append(sy_st[("v", h - 2)])
            R.op("scalar", (lambda p=p: (lambda e: e.copy(vs2[p][:], pB[:])))(),
                 waits=vsw, inc=sc, fence=True)
            sc_vs[h] = sc.n
            R.op("gpsimd", (lambda h=h, p=p: (lambda e: e.dma_start(
                out=v_sc[h * T:(h + 1) * T, :].rearrange("(tt p) d -> p tt d", p=128),
                in_=vs2[p][:].rearrange("p (tt d) -> p tt d", d=128))))(),
                waits=[(sc, sc_vs[h])], inc=s_qkv, incn=16)
            sy_st[("v", h)] = (s_qkv, s_qkv.n)
        a2_sc = sc.n
        a2_qkv = s_qkv.n
        R.phase = "a3"
        if DEV_STOP in ("q", "k", "v"):
            if DEV_STOP == "v":
                R.op("gpsimd", lambda e: e.dma_start(
                    out=out.rearrange("t e -> (t e)").rearrange("(r c) -> r c", c=2048),
                    in_=v_sc[:, :].rearrange("r d -> (r d)").rearrange("(r c) -> r c", c=2048)),
                    waits=[(s_qkv, a2_qkv)], inc=gp, incn=16)
            else:
                srcm = q_sc if DEV_STOP == "q" else k_sc
                R.op("gpsimd", (lambda srcm=srcm: (lambda e: e.dma_start(
                    out=out, in_=srcm[0:T, 0:E])))(),
                    waits=[(s_qkv, a2_qkv)], inc=gp, incn=16)
            R.ops["sync"].append(lambda e: [e.wait_ge(s.h, s.n) for s in R.all_sems if s.n])
            R.emit_all(block)
            es.close()
            return "STOP"

        head_pe = {}
        prev_aT_ve = 0
        prev_exp_sc = a2_sc
        for h in range(NH):
            p = h % 2
            ww = [(pe, head_pe[h - 2])] if h >= 2 else []
            if h == 0:
                ww.append((s_qkv, a2_qkv))
            R.op("sync", (lambda h=h, p=p: (lambda e: e.dma_start(
                out=qh[p][:], in_=q_sc[h * 128:(h + 1) * 128, :])))(),
                waits=ww, inc=s_hd[p], incn=16)
            R.op("sync", (lambda h=h, p=p: (lambda e: e.dma_start(
                out=kh[p][:], in_=k_sc[h * 128:(h + 1) * 128, :])))(),
                inc=s_hd[p], incn=16)
            R.op("sync", (lambda h=h, p=p: (lambda e: e.dma_start(
                out=vh[p][:].rearrange("p (kc d) -> p kc d", d=128),
                in_=v_sc[h * T:(h + 1) * T, :].rearrange("(kc p) d -> p kc d", p=128))))(),
                inc=s_hd[p], incn=16)
            hld = (s_hd[p], s_hd[p].n)
            for Hq in range(2):
                nkc = 8 * (Hq + 1)
                Sreg = [pA[:, 0:1024], pA[:, 1024:2048]]
                lps = pB[:, 0:1024]
                aps = pB[:, 1024:2048]
                exp_sc = {}
                for kc in range(nkc):
                    collo = max(0, 128 * kc - 1024 * Hq)
                    sq_ = kc % 2
                    pw = []
                    if kc == 0:
                        pw = [hld]
                    if kc < 2:
                        pw.append((sc, prev_exp_sc))
                    if kc >= 2:
                        pw.append((sc, exp_sc[kc - 2]))

                    def smk(kc, collo, sq_, Hq, p):
                        def f(e):
                            last = None
                            c0 = collo
                            while c0 < 1024:
                                c1 = min(1024, (c0 // 512 + 1) * 512)
                                last = e.matmul(Sreg[sq_][:, c0:c1],
                                                kh[p][:, kc * 128:(kc + 1) * 128],
                                                qh[p][:, Hq * 1024 + c0:Hq * 1024 + c1],
                                                start=True, stop=True)
                                c0 = c1
                            return last
                        return f
                    R.op("tensor", smk(kc, collo, sq_, Hq, p), waits=pw, inc=pe)
                    smm = pe.n
                    dcol = 128 * kc - 1024 * Hq
                    mwait = [(pe, smm)]
                    if 0 <= dcol < 1024:
                        R.op("vector", (lambda sq_=sq_, dcol=dcol: (lambda e: e.tensor_tensor(
                            Sreg[sq_][:, dcol:dcol + 128], Sreg[sq_][:, dcol:dcol + 128],
                            sb_maskT[:], ALU.add)))(),
                            waits=[(pe, smm)], inc=ve, fence=True)
                        mwait = [(ve, ve.n)]
                    R.op("scalar", (lambda kc=kc, collo=collo, sq_=sq_: (lambda e: e.activation(
                        ptil(kc)[:, collo:1024], Sreg[sq_][:, collo:1024], AF.Exp,
                        scale=INV_SQRT_D)))(), waits=mwait, inc=sc, fence=True)
                    exp_sc[kc] = sc.n
                exp_all = sc.n
                prev_exp_sc = exp_all

                def lavmk(nkc, Hq, p, which):
                    def f(e):
                        last = None
                        for kc in range(nkc):
                            collo = max(0, 128 * kc - 1024 * Hq)
                            c0 = collo
                            while c0 < 1024:
                                c1 = min(1024, (c0 // 512 + 1) * 512)
                                if which == 0:
                                    last = e.matmul(lps[:, c0:c1], sb_ones[:],
                                                    ptil(kc)[:, c0:c1],
                                                    start=(kc == 0), stop=(kc == nkc - 1))
                                else:
                                    last = e.matmul(
                                        aps[:, c0:c1],
                                        vh[p][:].rearrange("p (kc d) -> p kc d", d=128)[:, kc, :],
                                        ptil(kc)[:, c0:c1],
                                        start=(kc == 0), stop=(kc == nkc - 1))
                                c0 = c1
                        return last
                    return f
                R.op("tensor", lavmk(nkc, Hq, p, 0), waits=[(sc, exp_all)], inc=pe)
                l_pe = pe.n
                R.op("tensor", lavmk(nkc, Hq, p, 1),
                     waits=[(ve, prev_aT_ve)], inc=pe)
                av_pe = pe.n
                R.op("scalar", lambda e: e.copy(w_lsb[:], lps[:, 0:1024]),
                     waits=[(pe, l_pe)], inc=sc, fence=True)
                lr = sc.n
                R.op("vector", lambda e: e.reciprocal(w_lrc[:], w_lsb[:]),
                     waits=[(sc, lr)], inc=ve, fence=True)
                rc = ve.n
                R.op("vector", (lambda h=h, Hq=Hq: (lambda e: e.tensor_tensor(
                    aT[h][:, Hq * 1024:(Hq + 1) * 1024], aps, w_lrc[:], ALU.mult)))(),
                    waits=[(pe, av_pe)], inc=ve, fence=True)
                prev_aT_ve = ve.n
            head_pe[h] = pe.n
        a3_ve = ve.n
        R.phase = "a4"
        a4_map = {}
        for ec in range(ECN):
            ww = [(ve, a3_ve)] if ec == 0 else []
            R.op("gpsimd", (lambda ec=ec: (lambda e: e.dma_start(
                out=wob[ec][:], in_=woT[ec * 128:(ec + 1) * 128, :])))(),
                waits=ww, inc=s_wo, incn=16)
        wo_all = s_wo.n
        o_ve = {}
        for tt in range(TTN):
            r = tt % 2
            ww = [(ve, o_ve[tt - 2])] if tt >= 2 else []
            R.op("sync", (lambda tt=tt, r=r: (lambda e: e.dma_start(
                out=m_x2k[r][:], in_=x_src[tt * 128:(tt + 1) * 128, :])))(),
                waits=ww, inc=s_xl[r], incn=16)
            xld = (s_xl[r], s_xl[r].n)
            pw = [(ve, a3_ve)] if tt == 0 else []
            if tt >= 2:
                pw.append((ve, o_ve[tt - 2]))

            def omk(tt, r):
                def f(e):
                    last = None
                    if tt == 0:
                        e.wait_ge(s_wo.h, wo_all)
                    for n in range(4):
                        for ec in range(ECN):
                            last = e.matmul(pT_[r][:, n * 512:(n + 1) * 512],
                                            aT[ec][:, tt * 128:(tt + 1) * 128],
                                            wob[ec][:, n * 512:(n + 1) * 512],
                                            start=(ec == 0), stop=(ec == ECN - 1))
                    return last
                return f
            R.op("tensor", omk(tt, r), waits=pw, inc=pe)
            omm = pe.n
            R.op("vector", (lambda r=r: (lambda e: e.tensor_tensor(
                m_x2k[r][:], pT_[r][:], m_x2k[r][:], ALU.add)))(),
                waits=[(pe, omm), xld], inc=ve, fence=True)
            o_ve[tt] = ve.n
            stw = [(ve, o_ve[tt])]
            if tt >= 1:
                stw.append((s_xs, a4_map[tt - 1]))
            R.op("sync", (lambda tt=tt, r=r: (lambda e: e.dma_start(
                out=x_dst[tt * 128:(tt + 1) * 128, :], in_=m_x2k[r][:])))(),
                waits=stw, inc=s_xs, incn=16)
            a4_map[tt] = s_xs.n
        return a4_map

    R.phase = "ln1"
    ln_pre = ln_phase(x_in, spill=True)
    R.phase = "mat"
    gu_deferred = []
    gu_mat_ve, gu_store = mat_gu(ln_pre[1], gu_deferred)
    if DEV_STOP == "wg":
        for fn in gu_deferred:
            fn()
        R.op("gpsimd", lambda e: e.dma_start(out=out[:, 0:1536], in_=wg_p[0][:]),
             waits=[(ag, 8)], inc=gp, incn=16)
        R.op("gpsimd", lambda e: e.dma_start(out=out[:, 1536:2048],
                                             in_=wg_p[1][:, 0:512]),
             inc=gp, incn=16)
        R.ops["sync"].append(lambda e: [e.wait_ge(s.h, s.n) for s in R.all_sems if s.n])
        R.emit_all(block)
        es.close()
        return nc
    f1_dst = out if DEV_STOP in ("ffn1", "y0") else xb1
    ln_sc1, ln_ve1, xn_sp1 = ln_pre
    if DEV_STOP == "ln":
        for ec in range(ECN):
            R.op("gpsimd", (lambda ec=ec: (lambda e: e.dma_start(
                out=out[ec * 128:(ec + 1) * 128, :], in_=xnT[ec][:])))(),
                waits=[(sc, ln_sc1)] if ec == 0 else [], inc=gp, incn=16)
        R.ops["sync"].append(lambda e: [e.wait_ge(s.h, s.n) for s in R.all_sems if s.n])
        R.emit_all(block)
        es.close()
        return nc
    R.phase = "matd"
    d_store, mat_all_ve, d_ag_fn = mat_d(gu_mat_ve, gu_store)
    R.phase = "ffn1"
    hooks = {7: gu_deferred[0:2], 13: gu_deferred[2:4], 19: gu_deferred[4:6],
             32: [d_ag_fn]}
    f1_map = ffn_body(x_in, f1_dst, ln_sc1, ln_ve1, entry_ve=ln_ve1,
                      h32_extra=[(s_cd, d_store), (ve, mat_all_ve)],
                      gp_hooks=hooks)
    if f1_map == "STOP":
        return nc
    if DEV_STOP == "wd":
        R.op("gpsimd", lambda e: e.dma_start(out=out, in_=wd_sc[0:T, :]),
             waits=[(ag, 9)], inc=gp, incn=16)
        R.op("gpsimd", lambda e: e.dma_start(out=xb1, in_=wd_sc[O - T:O, :]),
             inc=gp, incn=16)
        R.ops["sync"].append(lambda e: [e.wait_ge(s.h, s.n) for s in R.all_sems if s.n])
        R.emit_all(block)
        es.close()
        return nc
    if DEV_STOP == "h":
        for c in range(16):
            R.op("gpsimd", (lambda c=c: (lambda e: e.dma_start(
                out=out[c * 128:(c + 1) * 128, 0:1024], in_=h_half(c)[:])))(),
                waits=[(ve, ve.n)] if c == 0 else [], inc=gp, incn=16)
        R.ops["sync"].append(lambda e: [e.wait_ge(s.h, s.n) for s in R.all_sems if s.n])
        R.emit_all(block)
        es.close()
        return nc
    if DEV_STOP not in ("ffn1", "y0"):
        a_dst = out if DEV_STOP == "attn" else xb2
        a4_map = attn_call(xb1, a_dst, f1_map)
        if a4_map == "STOP":
            return nc
        if DEV_STOP != "attn":
            R.phase = "ln2"
            ln_sc2, ln_ve2, xn_sp2 = ln_phase(xb2, spill=True, x_ready=s_xs.n)
            R.phase = "ffn2"
            ffn_body(xb2, out, ln_sc2, ln_ve2, entry_ve=ln_ve2)

    def final_waits(e):
        for s in R.all_sems:
            if s.n:
                e.wait_ge(s.h, s.n)
    R.ops["sync"].append(final_waits)

    R.emit_all(block)
    es.close()
    return nc


def _host_consts():
    ident = np.eye(128, dtype=np.float32).astype(ml_dtypes.bfloat16)
    ones = np.ones((128, 128), dtype=np.float32).astype(ml_dtypes.bfloat16)
    ones1 = np.ones((1, 128), dtype=np.float32)
    kk, qq = np.meshgrid(np.arange(128), np.arange(128), indexing="ij")
    maskT = np.where(kk <= qq, 0.0, -30000.0).astype(np.float32)
    inv = 1.0 / (1e6 ** (np.arange(0, HD, 2, dtype=np.float64) / HD))
    f = np.arange(T, dtype=np.float64)[:, None] * inv[None, :]
    cos = np.concatenate([np.cos(f), np.cos(f)], axis=1)
    sin = np.concatenate([np.sin(f), np.sin(f)], axis=1)
    cosT = np.ascontiguousarray(cos.T.astype(np.float32)).astype(ml_dtypes.bfloat16)
    sinT = np.ascontiguousarray(sin.T.astype(np.float32)).astype(ml_dtypes.bfloat16)
    rot64 = np.zeros((128, 128), dtype=np.float32)
    for pp in range(64):
        rot64[pp + 64, pp] = -1.0
    for pp in range(64, 128):
        rot64[pp - 64, pp] = 1.0
    rot64 = rot64.astype(ml_dtypes.bfloat16)
    return ident, ones, ones1, maskT, cosT, sinT, rot64


def kernel(L0_attn_x, L0_attn_wq, L0_attn_wk, L0_attn_wv, L0_attn_wo,
           L0_ffn_lut, L0_ffn_gate_idx, L0_ffn_up_idx, L0_ffn_down_idx):
    x = np.asarray(L0_attn_x, dtype=np.float32)
    wqT = np.ascontiguousarray(np.asarray(L0_attn_wq, dtype=np.float32).T)
    wkT = np.ascontiguousarray(np.asarray(L0_attn_wk, dtype=np.float32).T)
    wvT = np.ascontiguousarray(np.asarray(L0_attn_wv, dtype=np.float32).T)
    woT = np.ascontiguousarray(np.asarray(L0_attn_wo, dtype=np.float32).T)
    lut = np.asarray(L0_ffn_lut, dtype=np.float32)
    gidxT = np.ascontiguousarray(np.asarray(L0_ffn_gate_idx, dtype=np.int32).T)
    uidxT = np.ascontiguousarray(np.asarray(L0_ffn_up_idx, dtype=np.int32).T)
    didxT = np.ascontiguousarray(np.asarray(L0_ffn_down_idx, dtype=np.int32).T)
    lutb = np.ascontiguousarray(np.broadcast_to(lut[None, :], (128, 16)))
    ident, ones, ones1, maskT, cosT, sinTf, rot64 = _host_consts()

    nc = build_nc()
    in_maps = []
    for c in range(NCORES):
        in_maps.append({
            "x_in": np.ascontiguousarray(x[c]),
            "wqT": wqT, "wkT": wkT, "wvT": wvT, "woT": woT,
            "gpiece": np.ascontiguousarray(gidxT[c * EPC:(c + 1) * EPC, :]),
            "upiece": np.ascontiguousarray(uidxT[c * EPC:(c + 1) * EPC, :]),
            "dpiece": np.ascontiguousarray(didxT[c * OPC:(c + 1) * OPC, :]),
            "lutb": lutb,
            "c_ident": ident, "c_ones": ones, "c_ones1": ones1,
            "c_maskT": maskT, "c_cosT": cosT, "c_sinTf": sinTf,
            "c_rot64": rot64,
        })
    res = run_bass_kernel_spmd(nc, in_maps, core_ids=list(range(NCORES)))
    return np.stack([res.results[c]["out"] for c in range(NCORES)], axis=0)


# version 21 (final)
# speedup vs baseline: 1.0138x; 1.0138x over previous
import os
import numpy as np
import ml_dtypes
import concourse.bass as bass
import concourse.mybir as mybir
import concourse.bass_utils as _bass_utils
from concourse.bass_utils import run_bass_kernel_spmd

if not getattr(_bass_utils, "_mk_ldw_patched", False):
    _orig_run_command = _bass_utils.run_command

    def _mk_run_command(argv, **kwargs):
        argv = ["--enable-ldw-opt=true" if a == "--enable-ldw-opt=false" else a
                for a in argv]
        return _orig_run_command(argv, **kwargs)

    _bass_utils.run_command = _mk_run_command
    _bass_utils._mk_ldw_patched = True

F32 = mybir.dt.float32
BF16 = mybir.dt.bfloat16
I32 = mybir.dt.int32
AF = mybir.ActivationFunctionType
ALU = mybir.AluOpType

NCORES = 8
T = 2048
E = 2048
NH = 16
HD = 128
O = 5504
OC = O // 128
ECN = E // 128
TTN = T // 128
EPC = E // NCORES
OPC = O // NCORES
INV_SQRT_D = 1.0 / float(np.sqrt(HD))
KB = 1024

DEV_STOP = os.environ.get("MK_DEV_STOP", "")
DEV_SER = os.environ.get("MK_DEV_SER", "")


class Sem:
    def __init__(self, es, nc, name):
        self.h = es.enter_context(nc.semaphore(name))
        self.n = 0


class Rec:
    ENGS = ("sync", "gpsimd", "vector", "scalar", "tensor")

    def __init__(self):
        self.ops = {e: [] for e in self.ENGS}
        self.all_sems = []
        self.nc = None
        self.phase = "init"

    def op(self, eng, emit, waits=(), inc=None, incn=1, fence=False):
        if DEV_SER == "1":
            waits = list(waits) + [(s, s.n) for s in self.all_sems if s is not inc]
        elif DEV_SER == "sy" and eng == "sync" and inc is not None:
            waits = list(waits) + [(inc, inc.n)]
        wl = [(s.h, int(v)) for (s, v) in waits if v > 0]
        if inc is not None:
            inc.n += incn
        fb = self.fence_bufs.get(eng) if fence else None

        def run(e, wl=wl, emit=emit, inc=inc, incn=incn, fb=fb):
            for (h, v) in wl:
                e.wait_ge(h, v)
            inst = emit(e)
            if fb is not None:
                inst = e.tensor_copy(fb[0], fb[1]) if not isinstance(fb, tuple)                     else fb[0](e)
            if inc is not None:
                inst.then_inc(inc.h, incn)
        self.ops[eng].append(run)

    def emit_all(self, block):
        for eng in self.ENGS:
            lst = self.ops[eng]

            def f(e, lst=lst):
                for r in lst:
                    r(e)
            getattr(block, eng)(f)


def build_nc():
    nc = bass.Bass("TRN2", target_bir_lowering=False)

    def din(name, shape, dt):
        return nc.dram_tensor(name, list(shape), dt, kind="ExternalInput").ap()

    x_in = din("x_in", (T, E), F32)
    wqT = din("wqT", (E, E), F32)
    wkT = din("wkT", (E, E), F32)
    wvT = din("wvT", (E, E), F32)
    woT = din("woT", (E, E), F32)
    gpiece = din("gpiece", (EPC, O), I32)
    upiece = din("upiece", (EPC, O), I32)
    dpiece = din("dpiece", (OPC, E), I32)
    lutb_d = din("lutb", (128, 16), F32)
    c_ident = din("c_ident", (128, 128), BF16)
    c_ones = din("c_ones", (128, 128), BF16)
    c_ones1 = din("c_ones1", (1, 128), F32)
    c_maskT = din("c_maskT", (128, 128), F32)
    c_rot64 = din("c_rot64", (128, 128), BF16)
    c_cosT = din("c_cosT", (128, T), BF16)
    c_sinTf = din("c_sinTf", (128, T), BF16)

    out = nc.dram_tensor("out", [T, E], F32, kind="ExternalOutput").ap()

    xb1 = nc.dram_tensor("xb1", [T, E], F32).ap()
    xb2 = nc.dram_tensor("xb2", [T, E], F32).ap()
    xnT_sc = nc.dram_tensor("xnT_sc", [ECN * 128, T], BF16).ap()
    q_sc = nc.dram_tensor("q_sc", [NH * 128, T], BF16).ap()
    k_sc = nc.dram_tensor("k_sc", [NH * 128, T], BF16).ap()
    v_sc = nc.dram_tensor("v_sc", [NH * T, HD], BF16).ap()
    GUW = [1536, 1536, 1536, 896]
    cc_gp = [nc.dram_tensor(f"cc_g{j}", [EPC, w], BF16).ap()
             for j, w in enumerate(GUW)]
    cc_up = [nc.dram_tensor(f"cc_u{j}", [EPC, w], BF16).ap()
             for j, w in enumerate(GUW)]
    cc_d = nc.dram_tensor("cc_d", [OPC, E], BF16).ap()
    wg_p = [nc.dram_tensor(f"wg_sc{j}", [E, w], BF16, addr_space="Shared").ap()
            for j, w in enumerate(GUW)]
    wu_p = [nc.dram_tensor(f"wu_sc{j}", [E, w], BF16, addr_space="Shared").ap()
            for j, w in enumerate(GUW)]
    wd_sc = nc.dram_tensor("wd_sc", [O, E], BF16, addr_space="Shared").ap()

    from contextlib import ExitStack
    es = ExitStack()
    E_ = es.enter_context

    AB1 = 88 * KB
    AB2 = 88 * KB
    AMID = 16 * KB
    ACON = 3 * KB
    AWRK = 256 + 12 * KB
    a_b1 = E_(nc.sbuf_tensor("a_b1", [128, AB1], mybir.dt.uint8))
    a_b2 = E_(nc.sbuf_tensor("a_b2", [128, AB2], mybir.dt.uint8))
    a_mid = E_(nc.sbuf_tensor("a_mid", [128, AMID], mybir.dt.uint8))
    a_con = E_(nc.sbuf_tensor("a_con", [128, ACON], mybir.dt.uint8))
    a_wrk = E_(nc.sbuf_tensor("a_wrk", [128, AWRK], mybir.dt.uint8))

    def view(arena, off, dt, cols, parts=128):
        return arena[0:parts, off:off + cols * mybir.dt.size(dt)].bitcast(dt)

    sb_ident = view(a_con, 0, BF16, 128)
    sb_ones = view(a_con, 256, BF16, 128)
    sb_ones1 = view(a_con, 512, F32, 128, parts=1)
    sb_maskT = view(a_con, 1024, F32, 128)
    sb_lutb = view(a_con, 1536, F32, 16)
    sb_rot = view(a_con, 1792, BF16, 128)
    f_ve = view(a_con, 2048, F32, 4)
    f_sc = view(a_con, 2112, F32, 4)
    f_gp = view(a_con, 2176, F32, 4)

    w_stats = [view(a_wrk, i * 16, F32, 1) for i in range(16)]
    w_t1 = [view(a_wrk, 256 + q * 2 * KB, BF16, 1024) for q in range(2)]
    w_scrap = view(a_wrk, 256 + 4 * KB, F32, 1024)
    w_t2 = [view(a_wrk, 256 + 8 * KB + q * 2 * KB, BF16, 1024) for q in range(2)]

    m_wg = [view(a_mid, b * 8 * KB, BF16, 16 * 256) for b in range(2)]
    m_x2k = [view(a_mid, b * 8 * KB, F32, 2048) for b in range(2)]
    m_x1k = [view(a_mid, b * 4 * KB, F32, 1024) for b in range(2)]

    def h_half(c):
        return view(a_b1, c * 2 * KB, BF16, 1024)
    aT = [view(a_b1, h * 4 * KB, BF16, T) for h in range(NH)]
    sb_cos = view(a_b1, 64 * KB, BF16, T)
    sb_sin = view(a_b1, 68 * KB, BF16, T)
    w_lsb = view(a_b1, 72 * KB, F32, 1024)
    w_lrc = view(a_b1, 76 * KB, F32, 1024)

    xnT = [view(a_b2, ec * 4 * KB, BF16, T) for ec in range(ECN)]
    xnT_all = a_b2[0:128, 0:64 * KB].bitcast(BF16)
    m_wub = [view(a_b2, 64 * KB + b * 8 * KB, BF16, 16 * 256) for b in range(2)]
    xn_t = [view(a_b2, 80 * KB + p * 4 * KB, BF16, 2048) for p in range(2)]

    def wd_f(c):
        return view(a_b2, c * 2 * KB, BF16, 1024)
    def ptil(kc):
        return view(a_b2, kc * 2 * KB, BF16, 1024)
    qh = [view(a_b2, 32 * KB + p * 4 * KB, BF16, T) for p in range(2)]
    kh = [view(a_b2, 40 * KB + p * 4 * KB, BF16, T) for p in range(2)]
    vh = [view(a_b2, 48 * KB + p * 4 * KB, BF16, 16 * 128) for p in range(2)]
    w3 = [view(a_b2, 64 * KB + p * 12 * KB, BF16, 3 * 16 * 128) for p in range(2)]
    qraw = view(a_b1, 0, BF16, T)
    vtmp = view(a_b1, 4 * KB, BF16, T)
    qrot = view(a_b1, 8 * KB, BF16, T)
    qf = [view(a_b1, 12 * KB + i * 4 * KB, BF16, T) for i in range(2)]
    kf = [view(a_b1, 20 * KB + i * 4 * KB, BF16, T) for i in range(2)]
    vs2 = [view(a_b1, 28 * KB + i * 4 * KB, BF16, T) for i in range(2)]
    wob = [view(a_b2, ec * 4 * KB, BF16, T) for ec in range(ECN)]

    pA = E_(nc.psum_tensor("pA", [128, 2048], F32))
    pB = E_(nc.psum_tensor("pB", [128, 2048], F32))
    pT_ = [pA, pB]

    ag = Sem(es, nc, "ag")
    pe = Sem(es, nc, "pe")
    ve = Sem(es, nc, "ve")
    sc = Sem(es, nc, "sc")
    gp = Sem(es, nc, "gp")
    s_cn = Sem(es, nc, "s_cn")
    s_ig = Sem(es, nc, "s_ig")
    s_iu = Sem(es, nc, "s_iu")
    s_id = Sem(es, nc, "s_id")
    s_id2 = Sem(es, nc, "s_id2")
    s_cg = Sem(es, nc, "s_cg")
    s_cu = Sem(es, nc, "s_cu")
    s_cd = Sem(es, nc, "s_cd")
    s_xl = [Sem(es, nc, "s_xl0"), Sem(es, nc, "s_xl1")]
    s_sp = Sem(es, nc, "s_sp")
    s_rl = Sem(es, nc, "s_rl")
    s_wg = [Sem(es, nc, "s_wg0"), Sem(es, nc, "s_wg1")]
    s_wd = Sem(es, nc, "s_wd")
    s_xr = [Sem(es, nc, "s_xr0"), Sem(es, nc, "s_xr1")]
    s_xs = Sem(es, nc, "s_xs")
    s_w3 = [Sem(es, nc, "s_w30"), Sem(es, nc, "s_w31")]
    s_wo = Sem(es, nc, "s_wo")
    s_qkv = Sem(es, nc, "s_qkv")
    s_qv = Sem(es, nc, "s_qv")
    s_xl2 = [Sem(es, nc, "s_xl20"), Sem(es, nc, "s_xl21")]
    s_xs2 = Sem(es, nc, "s_xs2")
    s_hd = [Sem(es, nc, "s_hd0"), Sem(es, nc, "s_hd1")]

    R = Rec()
    R.nc = nc
    R.fence_bufs = {
        "vector": ((lambda e: e.tensor_copy(f_ve[:, 0:2], f_ve[:, 2:4])),),
        "scalar": ((lambda e: e.copy(f_sc[:, 0:2], f_sc[:, 2:4])),),
        "gpsimd": ((lambda e: e.tensor_copy(f_gp[:, 0:2], f_gp[:, 2:4])),),
    }
    R.all_sems = [ag, pe, ve, sc, gp, s_cn, s_ig, s_iu, s_id, s_id2, s_cg, s_cu, s_cd,
                  s_xl[0], s_xl[1], s_sp, s_rl, s_wg[0], s_wg[1], s_wd,
                  s_xr[0], s_xr[1], s_xs, s_w3[0], s_w3[1], s_wo, s_qkv,
                  s_hd[0], s_hd[1], s_qv, s_xl2[0], s_xl2[1], s_xs2]
    block = E_(nc.Block())

    for (dst, src) in [(sb_ident, c_ident), (sb_ones, c_ones), (sb_ones1, c_ones1),
                       (sb_maskT, c_maskT), (sb_lutb, lutb_d), (sb_rot, c_rot64)]:
        R.op("sync", (lambda d, s: (lambda e: e.dma_start(out=d, in_=s)))(dst, src),
             inc=s_cn, incn=16)
    consts_cn = s_cn.n
    R.op("vector", lambda e: e.memset(w_stats[11], 1e-5), inc=ve)

    def lookup_ops(e, src_i32, dst_bf, ib, scr, np_):
        last = None
        e.tensor_copy(ib[0:np_, :], src_i32[0:np_, :])
        for v in range(16):
            if v == 0:
                last = e.tensor_scalar(dst_bf[0:np_, :], ib[0:np_, :], 0.0,
                                       sb_lutb[0:np_, 0:1], ALU.is_equal, ALU.mult)
            else:
                e.tensor_scalar(scr[0:np_, :], ib[0:np_, :], float(v),
                                sb_lutb[0:np_, v:v + 1], ALU.is_equal, ALU.mult)
                last = e.tensor_add(dst_bf[0:np_, :], dst_bf[0:np_, :],
                                    scr[0:np_, :])
        return last

    GU_PARTS = [(0, 1536), (1536, 3072), (3072, 4608), (4608, O)]
    PW = 1536
    mb_idx = [view(a_b1, 64 * KB + q * 12 * KB, I32, 2 * PW) for q in range(2)]
    mw_ib = view(a_b2, 80 * KB, BF16, PW)
    mw_w = view(a_b2, 80 * KB + PW * 2, BF16, PW)
    mw_scr = w_scrap.bitcast(BF16)

    def lookup_ops(e, src_i32, dst_bf, ib, scr, np_):
        last = None
        e.tensor_copy(ib[0:np_, :], src_i32[0:np_, :])
        for v in range(16):
            if v == 0:
                last = e.tensor_scalar(dst_bf[0:np_, :], ib[0:np_, :], 0.0,
                                       sb_lutb[0:np_, 0:1], ALU.is_equal, ALU.mult)
            else:
                e.tensor_scalar(scr[0:np_, :], ib[0:np_, :], float(v),
                                sb_lutb[0:np_, v:v + 1], ALU.is_equal, ALU.mult)
                last = e.tensor_add(dst_bf[0:np_, :], dst_bf[0:np_, :],
                                    scr[0:np_, :])
        return last

    def mat_gu(ln_ve, deferred):
        seq = []
        for j, (c0, c1) in enumerate(GU_PARTS):
            seq.append(("g", j, c0, c1, gpiece, cc_gp[j], wg_p[j], s_cg))
            seq.append(("u", j, c0, c1, upiece, cc_up[j], wu_p[j], s_cu))
        lk_hist = {}
        prev_store = [None]
        ln_pe = pe.n
        ld_of = {}

        def idx_load(si, lw):
            mn, j, c0, c1, piece, cc, wsc_, scc = seq[si]
            q = si % 2
            w = c1 - c0
            siq = s_ig if q == 0 else s_iu
            R.op("sync", (lambda piece=piece, q=q, c0=c0, c1=c1, w=w:
                          (lambda e: e.dma_start(
                out=mb_idx[q].rearrange("p (a n) -> p a n", n=PW)[:, :, 0:w],
                in_=piece[:, c0:c1].rearrange("(a p) n -> p a n", p=128))))(),
                waits=lw, inc=siq, incn=16)
            ld_of[si] = (siq, siq.n)
        idx_load(0, [(s_cn, consts_cn)])
        idx_load(1, [])
        for si, (mn, j, c0, c1, piece, cc, wsc_, scc) in enumerate(seq):
            q = si % 2
            w = c1 - c0
            ld = ld_of[si]
            for ai in range(2):
                lkw = [ld] if ai == 0 else []
                if si == 0 and ai == 0:
                    lkw += [(ve, ln_ve), (pe, ln_pe)]
                if prev_store[0] is not None:
                    lkw.append(prev_store[0])
                R.op("vector", (lambda q=q, ai=ai, w=w: (lambda e: lookup_ops(
                    e, mb_idx[q].rearrange("p (a n) -> p a n", n=PW)[:, ai, 0:w],
                    mw_w[:, 0:w], mw_ib[:, 0:w], mw_scr[:, 0:w], 128)))(),
                    waits=lkw, inc=ve, fence=True)
                lk = ve.n
                R.op("sync", (lambda cc=cc, ai=ai, w=w:
                              (lambda e: e.dma_start(
                                  out=cc[ai * 128:(ai + 1) * 128, 0:w],
                                  in_=mw_w[:, 0:w])))(),
                    waits=[(ve, lk)], inc=scc, incn=16)
                prev_store[0] = (scc, scc.n)
            if si + 2 < len(seq):
                idx_load(si + 2, [(ve, ve.n)])
            lk_hist[si] = ve.n
            st = scc.n

            def mk_ag(cc=cc, wsc_=wsc_, scc=scc, st=st):
                R.op("gpsimd", lambda e: e.collective_compute(
                    "AllGather", ALU.bypass,
                    replica_groups=[list(range(NCORES))],
                    ins=[cc[:]], outs=[wsc_[:]]),
                    waits=[(scc, st)], inc=ag)
            if j == 0:
                mk_ag()
            else:
                deferred.append(mk_ag)
        return ve.n, prev_store[0]

    def mat_d(gu_ve, gu_store):
        db_ib = view(a_b2, 80 * KB, BF16, 2048)
        db_w = view(a_b2, 84 * KB, BF16, 2048)
        nstr = 6
        lk_hist = {}
        prev_store = gu_store
        dld = {}

        def d_load(i, lw):
            q = i % 2
            r0 = i * 128
            r1 = min(OPC, r0 + 128)
            nr = r1 - r0
            sdq = s_id if q == 0 else s_id2
            R.op("sync", (lambda q=q, r0=r0, r1=r1, nr=nr: (lambda e: e.dma_start(
                out=view(a_b1, 64 * KB + q * 12 * KB, I32, 2048)[0:nr, :],
                in_=dpiece[r0:r1, :])))(),
                waits=lw, inc=sdq, incn=16)
            dld[i] = (sdq, sdq.n)
        d_load(0, [(ve, gu_ve)])
        d_load(1, [])
        for i in range(nstr):
            q = i % 2
            r0 = i * 128
            r1 = min(OPC, r0 + 128)
            nr = r1 - r0
            R.op("vector", (lambda q=q, nr=nr: (lambda e: lookup_ops(
                e, view(a_b1, 64 * KB + q * 12 * KB, I32, 2048), db_w,
                db_ib, mw_scr, nr)))(),
                waits=[dld[i], prev_store], inc=ve, fence=True)
            lk_hist[i] = ve.n
            R.op("sync", (lambda r0=r0, r1=r1, nr=nr: (lambda e: e.dma_start(
                out=cc_d[r0:r1, :], in_=db_w[0:nr, :])))(),
                waits=[(ve, lk_hist[i])], inc=s_cd, incn=16)
            prev_store = (s_cd, s_cd.n)
            if i + 2 < nstr:
                d_load(i + 2, [(ve, lk_hist[i])])
        st = s_cd.n

        def d_ag():
            R.op("gpsimd", lambda e: e.collective_compute(
                "AllGather", ALU.bypass, replica_groups=[list(range(NCORES))],
                ins=[cc_d[:]], outs=[wd_sc[:]]),
                waits=[(s_cd, st)], inc=ag)
        return st, ve.n, d_ag

    def ln_phase(x_src, spill, evac_extra=None, x_ready=None):
        evac_sc = [0, 0]
        norm_ve = [0, 0]
        vst_v = [0, 0]
        pe_entry_ve = ve.n
        pe_entry_sc = sc.n
        for tt in range(TTN):
            p = tt % 2
            w = [(ve, norm_ve[p])] if tt >= 2 else []
            if tt == 0 and x_ready:
                w.append(x_ready)
            R.op("sync", (lambda tt=tt, p=p: (lambda e: e.dma_start(
                out=m_x2k[p][:, 0:1024],
                in_=x_src[tt * 128:(tt + 1) * 128, 0:1024])))(),
                waits=w, inc=s_xl[p], incn=16)
            wv = [(ve, norm_ve[p])] if tt >= 2 else []
            if tt == 0 and x_ready:
                wv.append(x_ready)
            R.op("gpsimd", (lambda tt=tt, p=p: (lambda e: e.dma_start(
                out=m_x2k[p][:, 1024:2048],
                in_=x_src[tt * 128:(tt + 1) * 128, 1024:2048])))(),
                waits=wv, inc=s_xl2[p], incn=16)
            ld = s_xl[p].n
            ld2 = s_xl2[p].n
            s_sum, s_sq, s_mean, s_var, s_rstd = (w_stats[p * 5 + i] for i in range(5))
            hi_sum, hi_sq = w_stats[12 + p], w_stats[14 + p]
            sw = [(s_xl[p], ld), (s_xl2[p], ld2)]
            if tt >= 2:
                sw.append((ve, norm_ve[p]))
            else:
                sw.append((ve, pe_entry_ve))
            def stats_s(e, p=p, s_sum=s_sum, s_sq=s_sq, s_mean=s_mean,
                        s_var=s_var, s_rstd=s_rstd, hi_sum=hi_sum, hi_sq=hi_sq):
                e.activation(w_scrap[:, 0:1024], m_x2k[p][:, 0:1024], AF.Identity,
                             accum_out=s_sum)
                e.activation(w_scrap[:, 0:1024], m_x2k[p][:, 1024:2048],
                             AF.Identity, accum_out=hi_sum)
                e.activation(w_scrap[:, 0:1024], m_x2k[p][:, 0:1024], AF.Square,
                             accum_out=s_sq)
                e.activation(w_scrap[:, 0:1024], m_x2k[p][:, 1024:2048],
                             AF.Square, accum_out=hi_sq)
                e.add(s_sum, s_sum, hi_sum)
                e.add(s_sq, s_sq, hi_sq)
                e.mul(s_mean, s_sum, 1.0 / E)
                e.mul(s_var, s_sq, 1.0 / E)
                e.activation(w_stats[10], s_mean, AF.Square)
                e.mul(w_stats[10], w_stats[10], -1.0)
                e.add(s_var, s_var, w_stats[10])
                e.add(s_var, s_var, w_stats[11])
                e.activation(w_stats[10], s_var, AF.Ln)
                return e.activation(s_rstd, w_stats[10], AF.Exp, scale=-0.5)
            R.op("scalar", (lambda f=stats_s: (lambda e: f(e)))(), waits=sw, inc=sc)
            rst = sc.n
            vst_v[p] = 0
            R.op("vector", (lambda p=p, s_mean=s_mean, s_rstd=s_rstd:
                            (lambda e: e.tensor_scalar(
                                xn_t[p][:], m_x2k[p][:], s_mean, s_rstd,
                                ALU.subtract, ALU.mult)))(),
                 waits=[(sc, rst)], inc=ve, fence=True)
            norm_ve[p] = ve.n
            w = [(ve, norm_ve[p])]
            if tt >= 2:
                w.append((sc, evac_sc[p]))
            elif pe_entry_ve or pe_entry_sc:
                w += [(ve, pe_entry_ve), (sc, pe_entry_sc)]
            def trans(e, p=p):
                last = None
                for ec in range(ECN):
                    last = e.matmul(pT_[p][:, ec * 128:(ec + 1) * 128],
                                    xn_t[p][:, ec * 128:(ec + 1) * 128],
                                    sb_ident[:], start=True, stop=True)
                return last
            R.op("tensor", (lambda f=trans: (lambda e: f(e)))(), waits=w, inc=pe)
            tmm = pe.n
            ew = [(pe, tmm)]
            if tt < 2 and evac_extra is not None:
                ew.append(evac_extra)
            def evac(e, p=p, tt=tt):
                src = pT_[p][:].rearrange("p (ec t) -> p ec t", t=128)
                dst = xnT_all.rearrange("p (ec t) -> p ec t", t=T)[
                    :, :, tt * 128:(tt + 1) * 128]
                return e.copy(dst, src)
            R.op("scalar", (lambda f=evac: (lambda e: f(e)))(), waits=ew, inc=sc,
                 fence=True)
            evac_sc[p] = sc.n
        ln_sc = sc.n
        ln_ve = ve.n
        if spill:
            for ec in range(ECN):
                R.op("sync", (lambda ec=ec: (lambda e: e.dma_start(
                    out=xnT_sc[ec * 128:(ec + 1) * 128, :], in_=xnT[ec][:])))(),
                    waits=[(sc, ln_sc)] if ec == 0 else [], inc=s_sp, incn=16)
        return ln_sc, ln_ve, s_sp.n

    def ffn_body(x_src, x_dst, ln_sc, ln_ve, entry_ve, h32_extra=None,
                 gp_hooks=None):
        for Hh in range(2):
            if Hh == 1:
                relw = [(pe, pe.n), (s_sp, s_sp.n)]
                for ec in range(ECN):
                    R.op("sync", (lambda ec=ec: (lambda e: e.dma_start(
                        out=xnT[ec][:], in_=xnT_sc[ec * 128:(ec + 1) * 128, :])))(),
                        waits=relw if ec == 0 else [], inc=s_rl, incn=16)
                entry_pe = [(ag, 2), (sc, sc.n), (ve, ve.n), (s_rl, s_rl.n)]
            else:
                entry_pe = [(ag, 2), (sc, ln_sc), (ve, entry_ve)]
            R.phase = R.phase.split("/")[0] + f"/gu{Hh}"
            NB = (OC + 1) // 2
            h_ve = {}
            wld = {}
            blk_pe = {}

            def load_gu(b):
                ww = []
                if b >= 2:
                    ww.append((pe, blk_pe[b - 2]))
                if b == 0:
                    ww.append((ve, ln_ve))
                    if Hh == 1:
                        ww.append((s_xs2, s_xs2.n))
                part = min(3, (2 * b + 1) // 12)
                ww.append((ag, 2 * (part + 1)))
                n0 = 2 * b * 128
                n1 = min(O, n0 + 256)
                wdt = n1 - n0
                cj0 = n0 - 1536 * part
                for (buf, wsc_) in ((m_wg, wg_p), (m_wub, wu_p)):
                    R.op("scalar", (lambda buf=buf, wt=wsc_[part], b=b,
                                  cj0=cj0, wdt=wdt: (lambda e: e.dma_start(
                        out=buf[b % 2][:].rearrange("p (a n) -> p a n", n=256)[:, :, 0:wdt],
                        in_=wt[:, cj0:cj0 + wdt].rearrange("(a p) n -> p a n", p=128))))(),
                        waits=ww, inc=s_wg[b % 2], incn=16)
                    ww = []
                wld[b] = (s_wg[b % 2], s_wg[b % 2].n)
            load_gu(0)
            load_gu(1)
            for c in range(OC):
                b, half = c // 2, c % 2
                q = c % 2
                pw = list(entry_pe) if c == 0 else []
                pw.append(wld[b])
                if c >= 2:
                    pw.append(h_ve[c - 2])
                def wmk(buf, b, half, q, ps_off, Hh):
                    def f(e):
                        wv = buf[b % 2][:].rearrange("p (a n) -> p a n", n=256)
                        last = None
                        for n in range(2):
                            for ec in range(ECN):
                                last = e.matmul(
                                    pT_[q][:, ps_off + n * 512:ps_off + n * 512 + 512],
                                    wv[:, ec, half * 128:half * 128 + 128],
                                    xnT[ec][:, Hh * 1024 + n * 512:Hh * 1024 + n * 512 + 512],
                                    start=(ec == 0), stop=(ec == ECN - 1))
                        return last
                    return f
                R.op("tensor", wmk(m_wg, b, half, q, 0, Hh), waits=pw, inc=pe)
                g_pe = pe.n
                R.op("tensor", wmk(m_wub, b, half, q, 1024, Hh), inc=pe)
                u_pe = pe.n
                blk_pe[b] = u_pe
                R.op("scalar", (lambda q=q: (lambda e: e.activation(
                    w_t1[q][:], pT_[q][:, 0:1024], AF.Silu)))(),
                    waits=[(pe, g_pe)], inc=sc, fence=True)
                R.op("scalar", (lambda q=q: (lambda e: e.copy(
                    w_t2[q][:], pT_[q][:, 1024:2048])))(),
                    waits=[(pe, u_pe)], inc=sc, fence=True)
                silu_sc = sc.n
                hww = [(sc, silu_sc)]
                if c == 32 and h32_extra is not None:
                    hww += h32_extra
                R.op("gpsimd", (lambda c=c, q=q: (lambda e: e.tensor_tensor(
                    h_half(c)[:], w_t1[q][:], w_t2[q][:], ALU.mult)))(),
                    waits=hww, inc=gp, incn=1, fence=True)
                h_ve[c] = (gp, gp.n)
                if half == 1 and b + 2 < NB:
                    load_gu(b + 2)
                if Hh == 0 and gp_hooks and c in gp_hooks:
                    for fn in gp_hooks[c]:
                        fn()
            gu_gp = gp.n
            R.phase = R.phase.split("/")[0] + f"/dn{Hh}"
            if DEV_STOP == "h0" and Hh == 0:
                for cch in range(16):
                    R.op("gpsimd", (lambda cch=cch: (lambda e: e.dma_start(
                        out=out[cch * 128:(cch + 1) * 128, 0:1024],
                        in_=h_half(cch)[:])))(),
                        waits=[(gp, gu_gp)] if cch == 0 else [], inc=gp, incn=16)
                R.ops["sync"].append(lambda e: [e.wait_ge(s.h, s.n) for s in R.all_sems if s.n])
                R.emit_all(block)
                es.close()
                return "STOP"
            last_psum = (gp, gu_gp)
            gate_end = pe.n
            prev_stage = None
            dn_st = {}
            for Ff in range(2):
                wd_sy = {}
                for c in range(OC):
                    ww = []
                    if prev_stage is None:
                        if c == 0:
                            ww = [(ag, 9), (pe, pe.n)]
                    else:
                        rel = {0: 11, 11: 22, 22: 33, 33: OC}.get(c)
                        if rel is not None:
                            ww = [(pe, prev_stage[rel])]
                    R.op("scalar", (lambda c=c, Ff=Ff: (lambda e: e.dma_start(
                        out=wd_f(c)[:],
                        in_=wd_sc[c * 128:(c + 1) * 128, Ff * 1024:(Ff + 1) * 1024])))(),
                        waits=ww, inc=s_wd, incn=16)
                    wd_sy[c] = s_wd.n
                y_ve = {}
                stage_pe = {}
                for tt in range(8):
                    r = tt % 2
                    gt = Hh * 8 + tt
                    g = Ff * 8 + tt
                    ww = ([(s_xs2, dn_st[g - 2])] if g >= 2
                          else [(pe, gate_end)])
                    R.op("sync", (lambda gt=gt, r=r, Ff=Ff: (lambda e: e.dma_start(
                        out=m_x1k[r][:],
                        in_=x_src[gt * 128:(gt + 1) * 128, Ff * 1024:(Ff + 1) * 1024])))(),
                        waits=ww, inc=s_xr[r], incn=16)
                    xld = (s_xr[r], s_xr[r].n)
                    pw = [(ve, y_ve[tt - 2])] if tt >= 2 else [last_psum]
                    def ymk(tt, r, wd_sy, c0, c1):
                        def f(e):
                            last = None
                            for c in range(c0, c1):
                                if tt == 0 and c0 == 0 and c % 4 == 0:
                                    e.wait_ge(s_wd.h, wd_sy[min(c + 3, OC - 1)])
                                for n in range(2):
                                    last = e.matmul(
                                        pT_[r][:, n * 512:(n + 1) * 512],
                                        h_half(c)[:, tt * 128:(tt + 1) * 128],
                                        wd_f(c)[:, n * 512:(n + 1) * 512],
                                        start=(c == 0), stop=(c == OC - 1))
                            return last
                        return f
                    if tt < 7:
                        R.op("tensor", ymk(tt, r, wd_sy, 0, OC), waits=pw, inc=pe)
                    else:
                        for (c0, c1) in ((0, 11), (11, 22), (22, 33), (33, OC)):
                            R.op("tensor", ymk(tt, r, wd_sy, c0, c1),
                                 waits=pw if c0 == 0 else [], inc=pe)
                            stage_pe[c1] = pe.n
                    ymm = pe.n
                    if DEV_STOP == "y0":
                        R.op("vector", (lambda r=r: (lambda e: e.tensor_scalar_mul(
                            m_x1k[r][:], pT_[r][:, 0:1024], 1.0)))(),
                            waits=[(pe, ymm), xld], inc=ve)
                    else:
                        R.op("vector", (lambda r=r: (lambda e: e.scalar_tensor_tensor(
                            m_x1k[r][:], pT_[r][:, 0:1024], 0.5, m_x1k[r][:],
                            ALU.mult, ALU.add)))(),
                            waits=[(pe, ymm), xld], inc=ve, fence=True)
                    y_ve[tt] = ve.n
                    stw = [(ve, y_ve[tt])]
                    if g >= 1:
                        stw.append((s_xs2, dn_st[g - 1]))
                    R.op("gpsimd", (lambda gt=gt, r=r, Ff=Ff: (lambda e: e.dma_start(
                        out=x_dst[gt * 128:(gt + 1) * 128, Ff * 1024:(Ff + 1) * 1024],
                        in_=m_x1k[r][:])))(),
                        waits=stw, inc=s_xs2, incn=16)
                    dn_st[g] = s_xs2.n
                last_psum = (ve, ve.n)
                prev_stage = stage_pe

    def attn_call(x_src, x_dst):
        R.phase = "aln"
        R.op("sync", lambda e: e.dma_start(out=sb_cos, in_=c_cosT),
             waits=[(pe, pe.n)], inc=s_cn, incn=16)
        R.op("sync", lambda e: e.dma_start(out=sb_sin, in_=c_sinTf),
             inc=s_cn, incn=16)
        rope_cn = s_cn.n
        ln_sc, ln_ve, _ = ln_phase(x_src, spill=False,
                                   x_ready=(s_xs2, s_xs2.n))
        R.phase = "a2"
        sc_vt = {}
        sc_vs = {}
        pe_v = {}
        ve_rope = {}
        sy_st = {}
        a2_entry_pe = pe.n
        wld_of = {}

        def load_w3(h):
            p = h % 2
            ww = [(pe, pe_v[h - 2])] if h >= 2 else [(pe, a2_entry_pe)]
            for wi, wt in enumerate((wqT, wkT, wvT)):
                R.op("gpsimd", (lambda wt=wt, p=p, wi=wi, h=h: (lambda e: e.dma_start(
                    out=w3[p][:].rearrange("p (w a n) -> p w a n", w=3, n=128)[:, wi],
                    in_=wt[:, h * 128:(h + 1) * 128].rearrange("(a p) n -> p a n", p=128))))(),
                    waits=ww if wi == 0 else [], inc=s_w3[p], incn=16)
                ww = []
            wld_of[h] = (s_w3[p], s_w3[p].n)
        load_w3(0)
        for h in range(NH):
            p = h % 2
            wld = wld_of[h]

            def pmk(p, wi, dst):
                def f(e):
                    wv = w3[p][:].rearrange("p (w a n) -> p w a n", w=3, n=128)
                    last = None
                    for n in range(4):
                        for ec in range(ECN):
                            last = e.matmul(dst[:, n * 512:(n + 1) * 512],
                                            wv[:, wi, ec, :],
                                            xnT[ec][:, n * 512:(n + 1) * 512],
                                            start=(ec == 0), stop=(ec == ECN - 1))
                    return last
                return f

            def rotmk(dst, srcb):
                def f(e):
                    last = None
                    for n in range(4):
                        last = e.matmul(dst[:, n * 512:(n + 1) * 512], sb_rot[:],
                                        srcb[:, n * 512:(n + 1) * 512],
                                        start=True, stop=True)
                    return last
                return f

            pw = [wld, (sc, sc_vt[h - 1] if h >= 1 else ln_sc)]
            R.op("tensor", pmk(p, 0, pA), waits=pw, inc=pe)
            q_pe = pe.n
            qw = [(pe, q_pe)]
            if h >= 1:
                qw.append((ve, ve_rope[("k", h - 1)]))
            R.op("scalar", lambda e: e.copy(qraw[:], pA[:]), waits=qw, inc=sc,
                 fence=True)
            qr = sc.n
            kw = [(sc, sc_vs[h - 1])] if h >= 1 else [(sc, ln_sc)]
            R.op("tensor", pmk(p, 1, pB), waits=kw, inc=pe)
            k_pe = pe.n
            if h + 1 < NH:
                load_w3(h + 1)
            R.op("tensor", rotmk(pA, qraw), waits=[(sc, qr)], inc=pe)
            rq_pe = pe.n
            R.op("scalar", lambda e: e.copy(qrot[:], pA[:]),
                 waits=[(pe, rq_pe)], inc=sc, fence=True)
            qrot_sc = sc.n
            def ropemk(dstf):
                def f(e):
                    e.tensor_tensor(vtmp[:], qrot[:], sb_sin[:], ALU.mult)
                    e.tensor_tensor(dstf[:], qraw[:], sb_cos[:], ALU.mult)
                    return e.tensor_add(dstf[:], dstf[:], vtmp[:])
                return f
            rw = [(sc, qrot_sc), (s_cn, rope_cn)]
            if h >= 1:
                rw.append((pe, pe_v[h - 1]))
            if h >= 2:
                rw.append(sy_st[("q", h - 2)])
            R.op("vector", ropemk(qf[p]), waits=rw, inc=ve, fence=True)
            ve_rope[("q", h)] = ve.n
            R.op("sync", (lambda h=h, p=p: (lambda e: e.dma_start(
                out=q_sc[h * 128:(h + 1) * 128, :], in_=qf[p][:])))(),
                waits=[(ve, ve.n), (s_qkv, s_qkv.n)], inc=s_qkv, incn=16)
            sy_st[("q", h)] = (s_qkv, s_qkv.n)
            R.op("scalar", lambda e: e.copy(qraw[:], pB[:]),
                 waits=[(pe, k_pe), (ve, ve_rope[("q", h)])], inc=sc, fence=True)
            kr = sc.n
            R.op("tensor", rotmk(pB, qraw), waits=[(sc, kr)], inc=pe)
            rk_pe = pe.n
            R.op("tensor", pmk(p, 2, pA), waits=[(sc, qrot_sc)], inc=pe)
            v_pe = pe.n
            R.op("scalar", lambda e: e.copy(qrot[:], pB[:]),
                 waits=[(pe, rk_pe), (ve, ve_rope[("q", h)])], inc=sc, fence=True)
            krot_sc = sc.n
            krw = [(sc, krot_sc)]
            if h >= 2:
                krw.append(sy_st[("k", h - 2)])
            R.op("vector", ropemk(kf[p]), waits=krw, inc=ve, fence=True)
            ve_rope[("k", h)] = ve.n
            R.op("sync", (lambda h=h, p=p: (lambda e: e.dma_start(
                out=k_sc[h * 128:(h + 1) * 128, :], in_=kf[p][:])))(),
                waits=[(ve, ve.n), (s_qkv, s_qkv.n)], inc=s_qkv, incn=16)
            sy_st[("k", h)] = (s_qkv, s_qkv.n)
            R.op("scalar", lambda e: e.copy(vtmp[:], pA[:]),
                 waits=[(pe, v_pe), (ve, ve_rope[("k", h)])], inc=sc, fence=True)
            sc_vt[h] = sc.n

            def vtrans(e):
                last = None
                for t2 in range(TTN):
                    last = e.matmul(pB[:, t2 * 128:(t2 + 1) * 128],
                                    vtmp[:, t2 * 128:(t2 + 1) * 128],
                                    sb_ident[:], start=True, stop=True)
                return last
            R.op("tensor", lambda e: vtrans(e),
                 waits=[(sc, sc_vt[h])], inc=pe)
            pe_v[h] = pe.n
            vsw = [(pe, pe_v[h])]
            if h >= 2:
                vsw.append(sy_st[("v", h - 2)])
            R.op("scalar", (lambda p=p: (lambda e: e.copy(vs2[p][:], pB[:])))(),
                 waits=vsw, inc=sc, fence=True)
            sc_vs[h] = sc.n
            R.op("gpsimd", (lambda h=h, p=p: (lambda e: e.dma_start(
                out=v_sc[h * T:(h + 1) * T, :].rearrange("(tt p) d -> p tt d", p=128),
                in_=vs2[p][:].rearrange("p (tt d) -> p tt d", d=128))))(),
                waits=[(sc, sc_vs[h]), (s_qv, s_qv.n)], inc=s_qv, incn=16)
            sy_st[("v", h)] = (s_qv, s_qv.n)
        a2_sc = sc.n
        a2_qkv = s_qkv.n
        a2_qv = s_qv.n
        R.phase = "a3"
        if DEV_STOP in ("q", "k", "v"):
            if DEV_STOP == "v":
                R.op("gpsimd", lambda e: e.dma_start(
                    out=out.rearrange("t e -> (t e)").rearrange("(r c) -> r c", c=2048),
                    in_=v_sc[:, :].rearrange("r d -> (r d)").rearrange("(r c) -> r c", c=2048)),
                    waits=[(s_qkv, a2_qkv), (s_qv, a2_qv)], inc=gp, incn=16)
            else:
                srcm = q_sc if DEV_STOP == "q" else k_sc
                R.op("gpsimd", (lambda srcm=srcm: (lambda e: e.dma_start(
                    out=out, in_=srcm[0:T, 0:E])))(),
                    waits=[(s_qkv, a2_qkv)], inc=gp, incn=16)
            R.ops["sync"].append(lambda e: [e.wait_ge(s.h, s.n) for s in R.all_sems if s.n])
            R.emit_all(block)
            es.close()
            return "STOP"

        head_pe = {}
        prev_aT_ve = 0
        prev_exp_sc = a2_sc
        for h in range(NH):
            p = h % 2
            ww = [(pe, head_pe[h - 2])] if h >= 2 else []
            if h == 0:
                ww.append((s_qkv, a2_qkv))
                ww.append((s_qv, a2_qv))
            R.op("sync", (lambda h=h, p=p: (lambda e: e.dma_start(
                out=qh[p][:], in_=q_sc[h * 128:(h + 1) * 128, :])))(),
                waits=ww, inc=s_hd[p], incn=16)
            R.op("sync", (lambda h=h, p=p: (lambda e: e.dma_start(
                out=kh[p][:], in_=k_sc[h * 128:(h + 1) * 128, :])))(),
                inc=s_hd[p], incn=16)
            R.op("sync", (lambda h=h, p=p: (lambda e: e.dma_start(
                out=vh[p][:].rearrange("p (kc d) -> p kc d", d=128),
                in_=v_sc[h * T:(h + 1) * T, :].rearrange("(kc p) d -> p kc d", p=128))))(),
                inc=s_hd[p], incn=16)
            hld = (s_hd[p], s_hd[p].n)
            for Hq in range(2):
                nkc = 8 * (Hq + 1)
                Sreg = [pA[:, 0:1024], pA[:, 1024:2048]]
                lps = pB[:, 0:1024]
                aps = pB[:, 1024:2048]
                exp_sc = {}
                for kc in range(nkc):
                    collo = max(0, 128 * kc - 1024 * Hq)
                    sq_ = kc % 2
                    pw = []
                    if kc == 0:
                        pw = [hld]
                    if kc < 2:
                        pw.append((sc, prev_exp_sc))
                    if kc >= 2:
                        pw.append((sc, exp_sc[kc - 2]))

                    def smk(kc, collo, sq_, Hq, p):
                        def f(e):
                            last = None
                            c0 = collo
                            while c0 < 1024:
                                c1 = min(1024, (c0 // 512 + 1) * 512)
                                last = e.matmul(Sreg[sq_][:, c0:c1],
                                                kh[p][:, kc * 128:(kc + 1) * 128],
                                                qh[p][:, Hq * 1024 + c0:Hq * 1024 + c1],
                                                start=True, stop=True)
                                c0 = c1
                            return last
                        return f
                    R.op("tensor", smk(kc, collo, sq_, Hq, p), waits=pw, inc=pe)
                    smm = pe.n
                    dcol = 128 * kc - 1024 * Hq
                    mwait = [(pe, smm)]
                    if 0 <= dcol < 1024:
                        R.op("vector", (lambda sq_=sq_, dcol=dcol: (lambda e: e.tensor_tensor(
                            Sreg[sq_][:, dcol:dcol + 128], Sreg[sq_][:, dcol:dcol + 128],
                            sb_maskT[:], ALU.add)))(),
                            waits=[(pe, smm)], inc=ve, fence=True)
                        mwait = [(ve, ve.n)]
                    R.op("scalar", (lambda kc=kc, collo=collo, sq_=sq_: (lambda e: e.activation(
                        ptil(kc)[:, collo:1024], Sreg[sq_][:, collo:1024], AF.Exp,
                        scale=INV_SQRT_D)))(), waits=mwait, inc=sc, fence=True)
                    exp_sc[kc] = sc.n
                exp_all = sc.n
                prev_exp_sc = exp_all

                def lavmk(nkc, Hq, p, which):
                    def f(e):
                        last = None
                        for kc in range(nkc):
                            collo = max(0, 128 * kc - 1024 * Hq)
                            c0 = collo
                            while c0 < 1024:
                                c1 = min(1024, (c0 // 512 + 1) * 512)
                                if which == 0:
                                    last = e.matmul(lps[:, c0:c1], sb_ones[:],
                                                    ptil(kc)[:, c0:c1],
                                                    start=(kc == 0), stop=(kc == nkc - 1))
                                else:
                                    last = e.matmul(
                                        aps[:, c0:c1],
                                        vh[p][:].rearrange("p (kc d) -> p kc d", d=128)[:, kc, :],
                                        ptil(kc)[:, c0:c1],
                                        start=(kc == 0), stop=(kc == nkc - 1))
                                c0 = c1
                        return last
                    return f
                R.op("tensor", lavmk(nkc, Hq, p, 0), waits=[(sc, exp_all)], inc=pe)
                l_pe = pe.n
                R.op("tensor", lavmk(nkc, Hq, p, 1),
                     waits=[(ve, prev_aT_ve)], inc=pe)
                av_pe = pe.n
                R.op("scalar", lambda e: e.copy(w_lsb[:], lps[:, 0:1024]),
                     waits=[(pe, l_pe)], inc=sc, fence=True)
                lr = sc.n
                R.op("vector", lambda e: e.reciprocal(w_lrc[:], w_lsb[:]),
                     waits=[(sc, lr)], inc=ve, fence=True)
                rc = ve.n
                R.op("vector", (lambda h=h, Hq=Hq: (lambda e: e.tensor_tensor(
                    aT[h][:, Hq * 1024:(Hq + 1) * 1024], aps, w_lrc[:], ALU.mult)))(),
                    waits=[(pe, av_pe)], inc=ve, fence=True)
                prev_aT_ve = ve.n
            head_pe[h] = pe.n
        a3_ve = ve.n
        R.phase = "a4"
        for ec in range(ECN):
            ww = [(ve, a3_ve)] if ec == 0 else []
            R.op("gpsimd", (lambda ec=ec: (lambda e: e.dma_start(
                out=wob[ec][:], in_=woT[ec * 128:(ec + 1) * 128, :])))(),
                waits=ww, inc=s_wo, incn=16)
        wo_all = s_wo.n
        o_ve = {}
        for tt in range(TTN):
            r = tt % 2
            ww = [(ve, o_ve[tt - 2])] if tt >= 2 else []
            R.op("sync", (lambda tt=tt, r=r: (lambda e: e.dma_start(
                out=m_x2k[r][:], in_=x_src[tt * 128:(tt + 1) * 128, :])))(),
                waits=ww, inc=s_xl[r], incn=16)
            xld = (s_xl[r], s_xl[r].n)
            pw = [(ve, a3_ve)] if tt == 0 else []
            if tt >= 2:
                pw.append((ve, o_ve[tt - 2]))

            def omk(tt, r):
                def f(e):
                    last = None
                    if tt == 0:
                        e.wait_ge(s_wo.h, wo_all)
                    for n in range(4):
                        for ec in range(ECN):
                            last = e.matmul(pT_[r][:, n * 512:(n + 1) * 512],
                                            aT[ec][:, tt * 128:(tt + 1) * 128],
                                            wob[ec][:, n * 512:(n + 1) * 512],
                                            start=(ec == 0), stop=(ec == ECN - 1))
                    return last
                return f
            R.op("tensor", omk(tt, r), waits=pw, inc=pe)
            omm = pe.n
            R.op("vector", (lambda r=r: (lambda e: e.tensor_tensor(
                m_x2k[r][:], pT_[r][:], m_x2k[r][:], ALU.add)))(),
                waits=[(pe, omm), xld], inc=ve, fence=True)
            o_ve[tt] = ve.n
            R.op("sync", (lambda tt=tt, r=r: (lambda e: e.dma_start(
                out=x_dst[tt * 128:(tt + 1) * 128, :], in_=m_x2k[r][:])))(),
                waits=[(ve, o_ve[tt])], inc=s_xs, incn=16)

    R.phase = "ln1"
    ln_pre = ln_phase(x_in, spill=True)
    R.phase = "mat"
    gu_deferred = []
    gu_mat_ve, gu_store = mat_gu(ln_pre[1], gu_deferred)
    if DEV_STOP == "wg":
        for fn in gu_deferred:
            fn()
        R.op("gpsimd", lambda e: e.dma_start(out=out[:, 0:1536], in_=wg_p[0][:]),
             waits=[(ag, 8)], inc=gp, incn=16)
        R.op("gpsimd", lambda e: e.dma_start(out=out[:, 1536:2048],
                                             in_=wg_p[1][:, 0:512]),
             inc=gp, incn=16)
        R.ops["sync"].append(lambda e: [e.wait_ge(s.h, s.n) for s in R.all_sems if s.n])
        R.emit_all(block)
        es.close()
        return nc
    f1_dst = out if DEV_STOP in ("ffn1", "y0") else xb1
    ln_sc1, ln_ve1, xn_sp1 = ln_pre
    if DEV_STOP == "ln":
        for ec in range(ECN):
            R.op("gpsimd", (lambda ec=ec: (lambda e: e.dma_start(
                out=out[ec * 128:(ec + 1) * 128, :], in_=xnT[ec][:])))(),
                waits=[(sc, ln_sc1)] if ec == 0 else [], inc=gp, incn=16)
        R.ops["sync"].append(lambda e: [e.wait_ge(s.h, s.n) for s in R.all_sems if s.n])
        R.emit_all(block)
        es.close()
        return nc
    R.phase = "matd"
    d_store, mat_all_ve, d_ag_fn = mat_d(gu_mat_ve, gu_store)
    R.phase = "ffn1"
    hooks = {5: gu_deferred[0:2], 11: gu_deferred[2:4], 17: gu_deferred[4:6],
             32: [d_ag_fn]}
    if ffn_body(x_in, f1_dst, ln_sc1, ln_ve1, entry_ve=ln_ve1,
                h32_extra=[(s_cd, d_store), (ve, mat_all_ve)],
                gp_hooks=hooks) == "STOP":
        return nc
    if DEV_STOP == "wd":
        R.op("gpsimd", lambda e: e.dma_start(out=out, in_=wd_sc[0:T, :]),
             waits=[(ag, 9)], inc=gp, incn=16)
        R.op("gpsimd", lambda e: e.dma_start(out=xb1, in_=wd_sc[O - T:O, :]),
             inc=gp, incn=16)
        R.ops["sync"].append(lambda e: [e.wait_ge(s.h, s.n) for s in R.all_sems if s.n])
        R.emit_all(block)
        es.close()
        return nc
    if DEV_STOP == "h":
        for c in range(16):
            R.op("gpsimd", (lambda c=c: (lambda e: e.dma_start(
                out=out[c * 128:(c + 1) * 128, 0:1024], in_=h_half(c)[:])))(),
                waits=[(ve, ve.n)] if c == 0 else [], inc=gp, incn=16)
        R.ops["sync"].append(lambda e: [e.wait_ge(s.h, s.n) for s in R.all_sems if s.n])
        R.emit_all(block)
        es.close()
        return nc
    if DEV_STOP not in ("ffn1", "y0"):
        a_dst = out if DEV_STOP == "attn" else xb2
        if attn_call(xb1, a_dst) == "STOP":
            return nc
        if DEV_STOP != "attn":
            R.phase = "ln2"
            ln_sc2, ln_ve2, xn_sp2 = ln_phase(xb2, spill=True,
                                              x_ready=(s_xs, s_xs.n))
            R.phase = "ffn2"
            ffn_body(xb2, out, ln_sc2, ln_ve2, entry_ve=ln_ve2)

    def final_waits(e):
        for s in R.all_sems:
            if s.n:
                e.wait_ge(s.h, s.n)
    R.ops["sync"].append(final_waits)

    R.emit_all(block)
    es.close()
    return nc


def _host_consts():
    ident = np.eye(128, dtype=np.float32).astype(ml_dtypes.bfloat16)
    ones = np.ones((128, 128), dtype=np.float32).astype(ml_dtypes.bfloat16)
    ones1 = np.ones((1, 128), dtype=np.float32)
    kk, qq = np.meshgrid(np.arange(128), np.arange(128), indexing="ij")
    maskT = np.where(kk <= qq, 0.0, -30000.0).astype(np.float32)
    inv = 1.0 / (1e6 ** (np.arange(0, HD, 2, dtype=np.float64) / HD))
    f = np.arange(T, dtype=np.float64)[:, None] * inv[None, :]
    cos = np.concatenate([np.cos(f), np.cos(f)], axis=1)
    sin = np.concatenate([np.sin(f), np.sin(f)], axis=1)
    cosT = np.ascontiguousarray(cos.T.astype(np.float32)).astype(ml_dtypes.bfloat16)
    sinT = np.ascontiguousarray(sin.T.astype(np.float32)).astype(ml_dtypes.bfloat16)
    rot64 = np.zeros((128, 128), dtype=np.float32)
    for pp in range(64):
        rot64[pp + 64, pp] = -1.0
    for pp in range(64, 128):
        rot64[pp - 64, pp] = 1.0
    rot64 = rot64.astype(ml_dtypes.bfloat16)
    return ident, ones, ones1, maskT, cosT, sinT, rot64


def kernel(L0_attn_x, L0_attn_wq, L0_attn_wk, L0_attn_wv, L0_attn_wo,
           L0_ffn_lut, L0_ffn_gate_idx, L0_ffn_up_idx, L0_ffn_down_idx):
    x = np.asarray(L0_attn_x, dtype=np.float32)
    wqT = np.ascontiguousarray(np.asarray(L0_attn_wq, dtype=np.float32).T)
    wkT = np.ascontiguousarray(np.asarray(L0_attn_wk, dtype=np.float32).T)
    wvT = np.ascontiguousarray(np.asarray(L0_attn_wv, dtype=np.float32).T)
    woT = np.ascontiguousarray(np.asarray(L0_attn_wo, dtype=np.float32).T)
    lut = np.asarray(L0_ffn_lut, dtype=np.float32)
    gidxT = np.ascontiguousarray(np.asarray(L0_ffn_gate_idx, dtype=np.int32).T)
    uidxT = np.ascontiguousarray(np.asarray(L0_ffn_up_idx, dtype=np.int32).T)
    didxT = np.ascontiguousarray(np.asarray(L0_ffn_down_idx, dtype=np.int32).T)
    lutb = np.ascontiguousarray(np.broadcast_to(lut[None, :], (128, 16)))
    ident, ones, ones1, maskT, cosT, sinTf, rot64 = _host_consts()

    nc = build_nc()
    in_maps = []
    for c in range(NCORES):
        in_maps.append({
            "x_in": np.ascontiguousarray(x[c]),
            "wqT": wqT, "wkT": wkT, "wvT": wvT, "woT": woT,
            "gpiece": np.ascontiguousarray(gidxT[c * EPC:(c + 1) * EPC, :]),
            "upiece": np.ascontiguousarray(uidxT[c * EPC:(c + 1) * EPC, :]),
            "dpiece": np.ascontiguousarray(didxT[c * OPC:(c + 1) * OPC, :]),
            "lutb": lutb,
            "c_ident": ident, "c_ones": ones, "c_ones1": ones1,
            "c_maskT": maskT, "c_cosT": cosT, "c_sinTf": sinTf,
            "c_rot64": rot64,
        })
    res = run_bass_kernel_spmd(nc, in_maps, core_ids=list(range(NCORES)))
    return np.stack([res.results[c]["out"] for c in range(NCORES)], axis=0)
